# version 24
# baseline (speedup 1.0000x reference)
import os
import numpy as np
from contextlib import ExitStack

import concourse.bass as bass
import concourse.mybir as mybir
from concourse.bass_utils import run_bass_kernel_spmd

F32 = mybir.dt.float32
BF16 = mybir.dt.bfloat16
I32 = mybir.dt.int32
AF = mybir.ActivationFunctionType
ALU = mybir.AluOpType
AX = mybir.AxisListType

T, D, H, DH, DFF = 2048, 1024, 16, 64, 4096
NT, NG, ND, NF = T // 128, T // 512, D // 128, DFF // 128
NEG = -1.0e30
ENGS = ("tensor", "vector", "scalar", "gpsimd", "sync")


class Sched:

    def __init__(self, nc, sems):
        self.nc = nc
        self.sems = sems
        self.cnt = {k: 0 for k in sems}
        self.w = {}
        self.r = {}
        self.clock = {e: {s: 0 for s in sems} for e in ENGS}
        self.pool = {"ld": [k for k in sems if k.startswith("ld")],
                     "st": [k for k in sems if k.startswith("st")]}
        self.pidx = {"ld": 0, "st": 0}
        self.progs = {e: [] for e in ENGS}

    def step(self, eng, emit, r=(), w=(), dma=None, n=1):
        issue_eng = {"ld": "sync", "st": "gpsimd"}.get(dma, eng)
        dsem = None
        if dma:
            pool = self.pool[dma]
            dsem = pool[self.pidx[dma] % len(pool)]
            self.pidx[dma] += 1
        deps = {}

        def add(sem, c):
            if sem == "tensor" and issue_eng == "tensor":
                return
            if c > deps.get(sem, 0):
                deps[sem] = c

        for b in r:
            d = self.w.get(id(b))
            if d:
                add(*d)
        for b in w:
            d = self.w.get(id(b))
            if d:
                add(*d)
            for sem, c in self.r.get(id(b), {}).items():
                add(sem, c)
        if dsem is not None and self.cnt[dsem] > deps.get(dsem, 0):
            deps[dsem] = self.cnt[dsem]
        clk = self.clock[issue_eng]
        waits = [(s, c) for s, c in deps.items() if clk[s] < c]
        for s, c in waits:
            clk[s] = c
        sem = dsem if dma else eng
        inc = 16 * n if dma else 1
        self.cnt[sem] += inc
        nexp = n

        def op(e, waits=waits, emit=emit, sem=sem, dma=dma, nexp=nexp):
            for s, c in waits:
                e.wait_ge(self.sems[s], c)
            insts = emit(e)
            if not isinstance(insts, (list, tuple)):
                insts = [insts]
            if dma:
                assert len(insts) == nexp, f"dma step declared n={nexp}, emitted {len(insts)}"
                for i in insts:
                    i.then_inc(self.sems[sem], 16)
            else:
                insts[-1].then_inc(self.sems[sem], 1)
        self.progs[issue_eng].append(op)

        end = self.cnt[sem]
        for b in w:
            self.w[id(b)] = (sem, end)
            self.r[id(b)] = {}
        for b in r:
            m = self.r.setdefault(id(b), {})
            m[sem] = max(m.get(sem, 0), end)

    def drain(self):
        def op(e):
            for s in self.sems:
                if self.cnt[s] > self.clock["sync"][s]:
                    e.wait_ge(self.sems[s], self.cnt[s])
                    self.clock["sync"][s] = self.cnt[s]
            e.nop()
        self.progs["sync"].append(op)

    def flush(self, blk):
        for engname in ENGS:
            ops = self.progs[engname]
            if not ops:
                continue

            def body(e, ops=tuple(ops)):
                for op in ops:
                    op(e)
            getattr(blk, engname)(body)
            self.progs[engname] = []


def build_program(debug=False, upto=99):
    p3skip = set(os.environ.get('K_P3SKIP', '').split(','))
    nc = bass.Bass("TRN2", target_bir_lowering=False)
    SK = "ExternalOutput" if debug else "Internal"

    x_h = nc.dram_tensor("x", [T, D], F32, kind="ExternalInput").ap()
    wq_h = nc.dram_tensor("wq", [D, D], F32, kind="ExternalInput").ap()
    wk_h = nc.dram_tensor("wk", [D, D], F32, kind="ExternalInput").ap()
    wv_h = nc.dram_tensor("wv", [D, D], F32, kind="ExternalInput").ap()
    wo_h = nc.dram_tensor("wo", [D, D], F32, kind="ExternalInput").ap()
    w1_h = nc.dram_tensor("w1", [DFF, D], F32, kind="ExternalInput").ap()
    cw_h = nc.dram_tensor("cw", [DFF, 3], F32, kind="ExternalInput").ap()
    cb_h = nc.dram_tensor("cb", [DFF], F32, kind="ExternalInput").ap()
    w2_h = nc.dram_tensor("w2", [D, DFF], F32, kind="ExternalInput").ap()
    out_h = nc.dram_tensor("out", [T, D], F32, kind="ExternalOutput").ap()

    wTp = {}
    for nm in "qkvo":
        wTp[nm] = (nc.dram_tensor(f"w{nm}Th", [D, D], BF16, kind=SK).ap(),
                   nc.dram_tensor(f"w{nm}Tl", [D, D], BF16, kind=SK).ap())
    w1qT_h = nc.dram_tensor("w1qT", [D, DFF], BF16, kind=SK).ap()
    w2qT_h = nc.dram_tensor("w2qT", [DFF, D], BF16, kind=SK).ap()
    QTp_h = (nc.dram_tensor("QTh", [D, T], BF16, kind=SK).ap(),
             nc.dram_tensor("QTl", [D, T], BF16, kind=SK).ap())
    KTp_h = (nc.dram_tensor("KTh", [D, T], BF16, kind=SK).ap(),
             nc.dram_tensor("KTl", [D, T], BF16, kind=SK).ap())
    VTp_h = (nc.dram_tensor("VTh", [D, T], BF16, kind=SK).ap(),
             nc.dram_tensor("VTl", [D, T], BF16, kind=SK).ap())
    ATT_h = nc.dram_tensor("ATT", [D, T], F32, kind=SK).ap()
    G_h = nc.dram_tensor("G", [DFF, T], F32, kind=SK).ap()
    OCTH_h = nc.dram_tensor("OCTH", [D, T], BF16, kind=SK).ap()
    OCTL_h = nc.dram_tensor("OCTL", [D, T], BF16, kind=SK).ap()
    R1ROW = nc.dram_tensor("R1ROW", [1, T], F32, kind=SK).ap()
    S2ROW = nc.dram_tensor("S2ROW", [1, T], F32, kind=SK).ap()
    RS2ROW = nc.dram_tensor("RS2ROW", [1, T], F32, kind=SK).ap()
    FFT_h = nc.dram_tensor("FFT", [D, T], F32, kind=SK).ap()

    top = ExitStack()
    e = top.enter_context

    eye = e(nc.sbuf_tensor("eye", [128, 128], F32))
    ebf = e(nc.sbuf_tensor("ebf", [128, 128], BF16))
    ones_c = e(nc.sbuf_tensor("ones_c", [128, 1], F32))
    ones_r = e(nc.sbuf_tensor("ones_r", [1, 128], F32))
    masks = [e(nc.sbuf_tensor(f"mask{r}", [128, 512], F32)) for r in range(4)]
    m1bc = e(nc.sbuf_tensor("m1bc", [128, 1], F32))
    m2bc = e(nc.sbuf_tensor("m2bc", [128, 1], F32))
    sw1bc = e(nc.sbuf_tensor("sw1bc", [128, 1], F32))
    sw2bc = e(nc.sbuf_tensor("sw2bc", [128, 1], F32))
    aT = [e(nc.sbuf_tensor(f"aT{i}", [128, T], BF16)) for i in range(ND)]
    ones_bf = e(nc.sbuf_tensor("ones_bf", [128, 1], BF16))
    zeros_bf = e(nc.sbuf_tensor("zeros_bf", [128, 1], BF16))


    stkI = ExitStack()
    iota_p = stkI.enter_context(nc.sbuf_tensor("iota_p", [128, 512], F32))
    iota_c = stkI.enter_context(nc.sbuf_tensor("iota_c", [128, 512], F32))

    psP = [e(nc.psum_tensor(f"psP{i}", [128, 512], F32)) for i in range(7)]
    psB16 = e(nc.psum_tensor("psB16", [128, 1024], BF16))

    semnames = (["tensor", "vector", "scalar", "gpsimd"]
                + [f"ld{i}" for i in range(16)] + [f"st{i}" for i in range(8)])
    sems = {n: e(nc.semaphore(f"sem_{n}")) for n in semnames}
    S = Sched(nc, sems)

    phase_no = [0]

    class Phase:
        def __init__(self):
            self.stack = ExitStack()
            phase_no[0] += 1
            self.skip = phase_no[0] > upto

        def __enter__(self):
            self.blk = nc.Block()
            self.blk.__enter__()
            return self.stack.enter_context

        def __exit__(self, *a):
            if self.skip:
                for eng in ENGS:
                    S.progs[eng] = []
            else:
                S.drain()
                S.flush(self.blk)
            self.blk.__exit__(*a)
            self.stack.close()
            return False

    with nc.Block() as blk0:
        def _clr(g):
            for s in sems.values():
                g.sem_clear(s)
        blk0.gpsimd(_clr)

    with Phase() as ph:
        S.step("gpsimd", lambda g: [
            g.iota(iota_p[:], [[0, 512]], channel_multiplier=1, allow_small_or_imprecise_dtypes=True),
            g.iota(iota_c[:], [[1, 512]], channel_multiplier=0, allow_small_or_imprecise_dtypes=True)],
            w=[iota_p, iota_c])
        S.step("vector", lambda v: v.tensor_tensor(eye[:], iota_p[:, 0:128], iota_c[:, 0:128], op=ALU.is_equal),
               r=[iota_p, iota_c], w=[eye])
        S.step("vector", lambda v: v.tensor_copy(ebf[:], eye[:]), r=[eye], w=[ebf])
        S.step("gpsimd", lambda g: g.memset(ones_c[:], 1.0), w=[ones_c])
        S.step("gpsimd", lambda g: g.memset(ones_r[:], 1.0), w=[ones_r])
        S.step("gpsimd", lambda g: g.memset(ones_bf[:], 1.0), w=[ones_bf])
        S.step("gpsimd", lambda g: g.memset(zeros_bf[:], 0.0), w=[zeros_bf])
        for rl in range(4):
            S.step("vector", lambda v, rl=rl: v.scalar_tensor_tensor(
                out=masks[rl][:], in0=iota_p[:], scalar=float(128 * rl), in1=iota_c[:],
                op0=ALU.add, op1=ALU.is_gt), r=[iota_p, iota_c], w=[masks[rl]])
            S.step("vector", lambda v, rl=rl: v.tensor_scalar(
                out=masks[rl][:], in0=masks[rl][:], scalar1=NEG, scalar2=None, op0=ALU.mult),
                r=[masks[rl]], w=[masks[rl]])

    stkI.close()

    with Phase() as ph:
        wsb = [ph(nc.sbuf_tensor(f"p1_w{k}", [128, D], F32)) for k in range(2)]
        whb = [ph(nc.sbuf_tensor(f"p1_wh{k}", [128, D], BF16)) for k in range(2)]
        wlb = [ph(nc.sbuf_tensor(f"p1_wl{k}", [128, D], BF16)) for k in range(2)]
        evb = [ph(nc.sbuf_tensor(f"p1_ev{k}", [128, 1024], BF16)) for k in range(3)]
        evi = [0]
        for nm, src_h in (("q", wq_h), ("k", wk_h), ("v", wv_h), ("o", wo_h)):
            dsth, dstl = wTp[nm]
            for rt in range(ND):
                wb, wh, wl = wsb[rt % 2], whb[rt % 2], wlb[rt % 2]
                S.step("ld", lambda s, src_h=src_h, rt=rt, wb=wb: s.dma_start(out=wb[:], in_=src_h[128 * rt:128 * (rt + 1), :]), r=[src_h], w=[wb], dma="ld")
                S.step("vector", lambda v, wb=wb, wh=wh: v.tensor_copy(wh[:], wb[:]), r=[wb], w=[wh])
                S.step("vector", lambda v, wb=wb, wh=wh, wl=wl: v.tensor_tensor(out=wl[:], in0=wb[:], in1=wh[:], op=ALU.subtract), r=[wb, wh], w=[wl])
                for mat, dst in ((wh, dsth), (wl, dstl)):
                    ev = evb[evi[0] % 3]
                    evi[0] += 1
                    S.step("tensor", lambda t, mat=mat: [
                        t.transpose(psB16[:, 128 * j:128 * (j + 1)], mat[:, 128 * j:128 * (j + 1)], ebf[:])
                        for j in range(ND)], r=[mat, ebf], w=[psB16])
                    S.step("vector", lambda v, ev=ev: v.tensor_copy(ev[:], psB16[:]), r=[psB16], w=[ev])
                    S.step("st", lambda s, dst=dst, rt=rt, ev=ev: [
                        s.dma_start(out=dst[128 * j:128 * (j + 1), 128 * rt:128 * (rt + 1)], in_=ev[:, 128 * j:128 * (j + 1)])
                        for j in range(ND)], r=[ev], w=[dst], dma="st", n=ND)

        wsb = [ph(nc.sbuf_tensor(f"p1b_w{k}", [128, DFF], F32)) for k in range(2)]
        red = [ph(nc.sbuf_tensor(f"p1b_red{k}", [128, 1], F32)) for k in range(2)]
        acc = ph(nc.sbuf_tensor("p1b_acc", [128, 1], F32))
        tot = ph(nc.sbuf_tensor("p1b_tot", [1, 1], F32))
        ma = [ph(nc.sbuf_tensor(f"p1b_ma{k}", [128, DFF], F32)) for k in range(2)]
        mbq = [ph(nc.sbuf_tensor(f"p1b_mb{k}", [128, DFF], F32)) for k in range(2)]
        mi = [ph(nc.sbuf_tensor(f"p1b_mi{k}", [128, DFF], BF16)) for k in range(2)]
        evb = [ph(nc.sbuf_tensor(f"p1b_ev{k}", [128, 512], BF16)) for k in range(2)]

        for (w_hh, nrow, ncol, dstT, mbc, swbc) in (
                (w1_h, NF, D, w1qT_h, m1bc, sw1bc),
                (w2_h, ND, DFF, w2qT_h, m2bc, sw2bc)):
            for i in range(nrow):
                wb, rd = wsb[i % 2], red[i % 2]
                S.step("ld", lambda s, w_hh=w_hh, i=i, ncol=ncol, wb=wb: s.dma_start(out=wb[:, 0:ncol], in_=w_hh[128 * i:128 * (i + 1), :]), r=[w_hh], w=[wb], dma="ld")
                S.step("vector", lambda v, ncol=ncol, wb=wb, rd=rd: v.tensor_reduce(out=rd[:], in_=wb[:, 0:ncol], axis=AX.X, op=ALU.add, apply_absolute_value=True), r=[wb], w=[rd])
                if i == 0:
                    S.step("vector", lambda v, rd=rd: v.tensor_copy(acc[:], rd[:]), r=[rd], w=[acc])
                else:
                    S.step("vector", lambda v, rd=rd: v.tensor_tensor(out=acc[:], in0=acc[:], in1=rd[:], op=ALU.add), r=[acc, rd], w=[acc])
            S.step("tensor", lambda t: t.matmul(psP[3][0:1, 0:1], ones_c[:], acc[:], start=True, stop=True), r=[ones_c, acc], w=[psP[3]])
            S.step("vector", lambda v, nrow=nrow, ncol=ncol: v.tensor_scalar(
                out=tot[:], in0=psP[3][0:1, 0:1], scalar1=1.0 / (nrow * 128 * ncol), scalar2=1e-5, op0=ALU.mult, op1=ALU.max), r=[psP[3]], w=[tot])
            S.step("tensor", lambda t: t.matmul(psP[3][0:128, 0:1], ones_r[:], tot[:], start=True, stop=True), r=[ones_r, tot], w=[psP[3]])
            S.step("vector", lambda v, mbc=mbc: v.tensor_copy(mbc[:], psP[3][0:128, 0:1]), r=[psP[3]], w=[mbc])
            S.step("vector", lambda v, mbc=mbc, swbc=swbc: v.reciprocal(swbc[:], mbc[:]), r=[mbc], w=[swbc])
            for i in range(nrow):
                wb, maq, mbb, miq = wsb[i % 2], ma[i % 2], mbq[i % 2], mi[i % 2]
                S.step("ld", lambda s, w_hh=w_hh, i=i, ncol=ncol, wb=wb: s.dma_start(out=wb[:, 0:ncol], in_=w_hh[128 * i:128 * (i + 1), :]), r=[w_hh], w=[wb], dma="ld")
                S.step("vector", lambda g, ncol=ncol, swbc=swbc, wb=wb, maq=maq: g.tensor_scalar(
                    out=maq[:, 0:ncol], in0=wb[:, 0:ncol], scalar1=swbc[:], scalar2=0.5, op0=ALU.mult, op1=ALU.is_ge), r=[wb, swbc], w=[maq])
                S.step("vector", lambda g, ncol=ncol, swbc=swbc, wb=wb, mbb=mbb: g.tensor_scalar(
                    out=mbb[:, 0:ncol], in0=wb[:, 0:ncol], scalar1=swbc[:], scalar2=-0.5, op0=ALU.mult, op1=ALU.is_le), r=[wb, swbc], w=[mbb])
                S.step("vector", lambda v, ncol=ncol, maq=maq, mbb=mbb, miq=miq: v.tensor_tensor(
                    out=miq[:, 0:ncol], in0=maq[:, 0:ncol], in1=mbb[:, 0:ncol], op=ALU.subtract), r=[maq, mbb], w=[miq])
                for q in range(ncol // 512):
                    ev = evb[q % 2]
                    def ftr(t, q=q, miq=miq):
                        return [t.transpose(psB16[:, 128 * j:128 * (j + 1)],
                                            miq[:, 512 * q + 128 * j:512 * q + 128 * (j + 1)],
                                            ebf[:]) for j in range(4)]
                    S.step("tensor", ftr, r=[miq, ebf], w=[psB16])
                    S.step("vector", lambda vv, ev=ev: vv.tensor_copy(ev[:], psB16[:, 0:512]), r=[psB16], w=[ev])
                    def fst(s, dstT=dstT, q=q, i=i, ev=ev):
                        return [s.dma_start(
                            out=dstT[512 * q + 128 * j:512 * q + 128 * (j + 1), 128 * i:128 * (i + 1)],
                            in_=ev[:, 128 * j:128 * (j + 1)]) for j in range(4)]
                    S.step("st", fst, r=[ev], w=[dstT], dma="st", n=4)

    stkA = ExitStack()
    xnTh = [stkA.enter_context(nc.sbuf_tensor(f"xnTh{i}", [128, T], BF16)) for i in range(ND)]
    xnTl = [stkA.enter_context(nc.sbuf_tensor(f"xnTl{i}", [128, T], BF16)) for i in range(ND)]
    with Phase() as ph:
        NB = 2
        xsb = [ph(nc.sbuf_tensor(f"p0_x{k}", [128, D], F32)) for k in range(NB)]
        sqsb = [ph(nc.sbuf_tensor(f"p0_sq{k}", [128, D], F32)) for k in range(NB)]
        xn = [ph(nc.sbuf_tensor(f"p0_xn{k}", [128, D], F32)) for k in range(NB)]
        usb = [ph(nc.sbuf_tensor(f"p0_u{k}", [128, D], F32)) for k in range(NB)]
        cint = [ph(nc.sbuf_tensor(f"p0_ci{k}", [128, D], I32)) for k in range(NB)]
        cm1 = [ph(nc.sbuf_tensor(f"p0_cm1{k}", [128, D], F32)) for k in range(NB)]
        dsb = [ph(nc.sbuf_tensor(f"p0_d{k}", [128, D], F32)) for k in range(NB)]
        ainb = [ph(nc.sbuf_tensor(f"p0_ai{k}", [128, D], BF16)) for k in range(NB)]
        xnhb = [ph(nc.sbuf_tensor(f"p0_xnh{k}", [128, D], BF16)) for k in range(NB)]
        xnlb = [ph(nc.sbuf_tensor(f"p0_xnl{k}", [128, D], BF16)) for k in range(NB)]
        st8 = [ph(nc.sbuf_tensor(f"p0_st8{k}", [128, 8], F32)) for k in range(NB)]
        s1b = [ph(nc.sbuf_tensor(f"p0_s1{k}", [128, 1], F32)) for k in range(NB)]
        rs1b = [ph(nc.sbuf_tensor(f"p0_rs1{k}", [128, 1], F32)) for k in range(NB)]
        rcol = ph(nc.sbuf_tensor("p0_rcol", [128, 16], F32))
        rowp = ph(nc.sbuf_tensor("p0_rowp", [16, 128], F32))

        for tt in range(NT):
            k = tt % NB
            x_, sq_, xn_, u_, ci_, cm_, d_, ai_ = xsb[k], sqsb[k], xn[k], usb[k], cint[k], cm1[k], dsb[k], ainb[k]
            stat = st8[k]
            ssum, ssq, mean, m2v, var, stdv, rstd, amax = [stat[:, i:i + 1] for i in range(8)]
            s1, rs1 = s1b[k], rs1b[k]
            V = "vector"
            S.step("ld", lambda s, tt=tt, x_=x_: s.dma_start(out=x_[:], in_=x_h[128 * tt:128 * (tt + 1), :]), r=[x_h], w=[x_], dma="ld")
            S.step(V, lambda v, x_=x_, stat=stat: v.tensor_reduce(out=stat[:, 0:1], in_=x_[:], axis=AX.X, op=ALU.add), r=[x_], w=[stat])
            S.step("vector", lambda v, x_=x_, sq_=sq_: v.tensor_tensor(out=sq_[:], in0=x_[:], in1=x_[:], op=ALU.mult), r=[x_], w=[sq_])
            S.step(V, lambda v, sq_=sq_, stat=stat: v.tensor_reduce(out=stat[:, 1:2], in_=sq_[:], axis=AX.X, op=ALU.add), r=[sq_, stat], w=[stat])
            S.step(V, lambda v, stat=stat: v.tensor_scalar(out=stat[:, 2:3], in0=stat[:, 0:1], scalar1=1.0 / D, scalar2=None, op0=ALU.mult), r=[stat], w=[stat])
            S.step(V, lambda v, stat=stat: v.tensor_tensor(out=stat[:, 3:4], in0=stat[:, 2:3], in1=stat[:, 2:3], op=ALU.mult), r=[stat], w=[stat])
            S.step(V, lambda v, stat=stat: v.scalar_tensor_tensor(out=stat[:, 4:5], in0=stat[:, 1:2], scalar=1.0 / D, in1=stat[:, 3:4], op0=ALU.mult, op1=ALU.subtract), r=[stat], w=[stat])
            S.step(V, lambda v, stat=stat: v.tensor_scalar(out=stat[:, 4:5], in0=stat[:, 4:5], scalar1=1e-5, scalar2=None, op0=ALU.add), r=[stat], w=[stat])
            S.step("scalar", lambda s, stat=stat: s.activation(out=stat[:, 5:6], in_=stat[:, 4:5], func=AF.Sqrt), r=[stat], w=[stat])
            S.step(V, lambda v, stat=stat: v.reciprocal(stat[:, 6:7], stat[:, 5:6]), r=[stat], w=[stat])
            S.step(V, lambda v, x_=x_, xn_=xn_, stat=stat: v.tensor_scalar(out=xn_[:], in0=x_[:], scalar1=stat[:, 2:3], scalar2=stat[:, 6:7], op0=ALU.subtract, op1=ALU.mult), r=[x_, stat], w=[xn_])
            S.step(V, lambda v, xn_=xn_, stat=stat: v.tensor_reduce(out=stat[:, 7:8], in_=xn_[:], axis=AX.X, op=ALU.max, apply_absolute_value=True), r=[xn_, stat], w=[stat])
            S.step(V, lambda v, stat=stat, s1=s1: v.tensor_scalar(out=s1[:], in0=stat[:, 7:8], scalar1=1e-5, scalar2=None, op0=ALU.max), r=[stat], w=[s1])
            S.step(V, lambda v, s1=s1: v.reciprocal(s1[:], s1[:]), r=[s1], w=[s1])
            S.step(V, lambda v, s1=s1: v.tensor_scalar(out=s1[:], in0=s1[:], scalar1=127.0, scalar2=None, op0=ALU.mult), r=[s1], w=[s1])
            S.step(V, lambda v, s1=s1, rs1=rs1: v.reciprocal(rs1[:], s1[:]), r=[s1], w=[rs1])
            S.step(V, lambda v, rs1=rs1, tt=tt: v.tensor_copy(rcol[:, tt:tt + 1], rs1[:]), r=[rs1, rcol], w=[rcol])
            S.step(V, lambda v, xn_=xn_, u_=u_, s1=s1: v.tensor_scalar(out=u_[:], in0=xn_[:], scalar1=s1[:], scalar2=None, op0=ALU.mult), r=[xn_, s1], w=[u_])
            S.step(V, lambda v, u_=u_, ci_=ci_: v.tensor_copy(ci_[:], u_[:]), r=[u_], w=[ci_])
            S.step(V, lambda v, ci_=ci_, cm_=cm_: v.tensor_scalar(out=cm_[:], in0=ci_[:], scalar1=-1.0, scalar2=None, op0=ALU.add), r=[ci_], w=[cm_])
            S.step(V, lambda v, u_=u_, cm_=cm_, d_=d_: v.scalar_tensor_tensor(out=d_[:], in0=u_[:], scalar=-1.0, in1=cm_[:], op0=ALU.add, op1=ALU.subtract), r=[u_, cm_], w=[d_])
            S.step(V, lambda v, d_=d_, cm_=cm_, u_=u_: v.scalar_tensor_tensor(out=u_[:], in0=d_[:], scalar=0.5, in1=cm_[:], op0=ALU.is_ge, op1=ALU.add), r=[d_, cm_], w=[u_])
            S.step(V, lambda v, d_=d_, u_=u_, ai_=ai_: v.scalar_tensor_tensor(out=ai_[:], in0=d_[:], scalar=-0.5, in1=u_[:], op0=ALU.is_gt, op1=ALU.add), r=[d_, u_], w=[ai_])
            xnh_, xnl_ = xnhb[k], xnlb[k]
            S.step("vector", lambda v, xn_=xn_, xnh_=xnh_: v.tensor_copy(xnh_[:], xn_[:]), r=[xn_], w=[xnh_])
            S.step("vector", lambda v, xn_=xn_, xnh_=xnh_, xnl_=xnl_: v.tensor_tensor(out=xnl_[:], in0=xn_[:], in1=xnh_[:], op=ALU.subtract), r=[xn_, xnh_], w=[xnl_])
            for mat, dstl in ((xnh_, xnTh), (xnl_, xnTl), (ai_, aT)):
                S.step("tensor", lambda t, mat=mat: [
                    t.transpose(psB16[:, 128 * j:128 * (j + 1)], mat[:, 128 * j:128 * (j + 1)], ebf[:])
                    for j in range(ND)], r=[mat, ebf], w=[psB16])
                S.step("vector", lambda v, dstl=dstl, tt=tt: [
                    v.tensor_copy(dstl[j][:, 128 * tt:128 * (tt + 1)], psB16[:, 128 * j:128 * (j + 1)])
                    for j in range(ND)], r=[psB16] + dstl, w=dstl)
        S.step("tensor", lambda t: t.transpose(psP[2][0:16, 0:128], rcol[:], eye[:]), r=[rcol, eye], w=[psP[2]])
        S.step("vector", lambda v: v.tensor_copy(rowp[:], psP[2][0:16, 0:128]), r=[psP[2]], w=[rowp])
        S.step("st", lambda s: [s.dma_start(out=R1ROW[0:1, 128 * j:128 * (j + 1)], in_=rowp[j:j + 1, :]) for j in range(16)], r=[rowp], w=[R1ROW], dma="st", n=16)

    with Phase() as ph:
        shb = [[ph(nc.sbuf_tensor(f"p2_sh{k}_{i}", [128, 128], BF16)) for i in range(ND)] for k in range(2)]
        slb = [[ph(nc.sbuf_tensor(f"p2_sl{k}_{i}", [128, 128], BF16)) for i in range(ND)] for k in range(2)]
        evh = [ph(nc.sbuf_tensor(f"p2_evh{k}", [128, 512], BF16)) for k in range(3)]
        evl = [ph(nc.sbuf_tensor(f"p2_evl{k}", [128, 512], BF16)) for k in range(3)]
        n = 0
        for nm, dstp in (("q", QTp_h), ("k", KTp_h), ("v", VTp_h)):
            srch, srcl = wTp[nm]
            for of in range(ND):
                sh, sl = shb[n % 2], slb[n % 2]
                n += 1
                S.step("ld", lambda s, srch=srch, srcl=srcl, of=of, sh=sh, sl=sl: [
                    s.dma_start(out=sh[i][:], in_=srch[128 * i:128 * (i + 1), 128 * of:128 * (of + 1)])
                    for i in range(ND)] + [
                    s.dma_start(out=sl[i][:], in_=srcl[128 * i:128 * (i + 1), 128 * of:128 * (of + 1)])
                    for i in range(ND)], r=[srch, srcl], w=sh + sl, dma="ld", n=2 * ND)
                for tg in range(NG):
                    ps = psP[tg % 2]
                    def fmm(t, tg=tg, sh=sh, sl=sl, ps=ps):
                        mm = []
                        for i in range(ND):
                            mm.append(t.matmul(ps[:], sh[i][:], xnTh[i][:, 512 * tg:512 * (tg + 1)], start=(i == 0), stop=False))
                            mm.append(t.matmul(ps[:], sh[i][:], xnTl[i][:, 512 * tg:512 * (tg + 1)], start=False, stop=False))
                            mm.append(t.matmul(ps[:], sl[i][:], xnTh[i][:, 512 * tg:512 * (tg + 1)], start=False, stop=(i == ND - 1)))
                        return mm
                    S.step("tensor", fmm, r=sh + sl + xnTh + xnTl, w=[ps])
                    eh, el = evh[tg % 3], evl[tg % 3]
                    S.step("vector", lambda v, ps=ps, eh=eh: v.tensor_copy(eh[:], ps[:]), r=[ps], w=[eh])
                    S.step("vector", lambda v, ps=ps, eh=eh, el=el: v.tensor_tensor(out=el[:], in0=ps[:], in1=eh[:], op=ALU.subtract), r=[ps, eh], w=[el])
                    S.step("st", lambda s, dstp=dstp, of=of, tg=tg, eh=eh, el=el: [
                        s.dma_start(out=dstp[0][128 * of:128 * (of + 1), 512 * tg:512 * (tg + 1)], in_=eh[:]),
                        s.dma_start(out=dstp[1][128 * of:128 * (of + 1), 512 * tg:512 * (tg + 1)], in_=el[:])],
                        r=[eh, el], w=[dstp[0], dstp[1]], dma="st", n=2)
    stkA.close()

    with Phase() as ph:
        NBH = 2
        K2A = [ph(nc.sbuf_tensor(f"p3_k2a{k}", [128, T], BF16)) for k in range(NBH)]
        K2B = [ph(nc.sbuf_tensor(f"p3_k2b{k}", [128, T], BF16)) for k in range(NBH)]
        Q2A = [ph(nc.sbuf_tensor(f"p3_q2a{k}", [128, T], BF16)) for k in range(NBH)]
        Q2B = [ph(nc.sbuf_tensor(f"p3_q2b{k}", [128, T], BF16)) for k in range(NBH)]
        QLA = [ph(nc.sbuf_tensor(f"p3_qla{k}", [128, T], BF16)) for k in range(NBH)]
        QLB = [ph(nc.sbuf_tensor(f"p3_qlb{k}", [128, T], BF16)) for k in range(NBH)]
        vph = ph(nc.sbuf_tensor("p3_vh", [128, T], BF16))
        vpl = ph(nc.sbuf_tensor("p3_vl", [128, T], BF16))
        sq2 = ph(nc.sbuf_tensor("p3_sq", [128, T], BF16))
        stb = [[ph(nc.sbuf_tensor(f"p3_st{h2}{k}", [1, 16], F32)) for h2 in range(2)] for k in range(NBH)]
        negMb = [[ph(nc.sbuf_tensor(f"p3_nM{h2}{k}", [128, 1], F32)) for h2 in range(2)] for k in range(NBH)]
        vchb = [[[ph(nc.sbuf_tensor(f"p3_vch{h2}{p}_{i}", [128, 65], BF16)) for i in range(NT)]
                 for p in range(2)] for h2 in range(2)]
        ptf = [ph(nc.sbuf_tensor(f"p3_ptf{k}", [128, 512], F32)) for k in range(4)]
        pth = [ph(nc.sbuf_tensor(f"p3_pth{k}", [128, 512], BF16)) for k in range(4)]
        ptl = [ph(nc.sbuf_tensor(f"p3_ptl{k}", [128, 512], BF16)) for k in range(4)]
        rsb = [ph(nc.sbuf_tensor(f"p3_rs{k}", [1, 512], F32)) for k in range(2)]
        rrb = [ph(nc.sbuf_tensor(f"p3_rr{k}", [64, 512], F32)) for k in range(2)]
        otm = [ph(nc.sbuf_tensor(f"p3_ot{k}", [64, 512], F32)) for k in range(2)]
        othi = [ph(nc.sbuf_tensor(f"p3_oth{k}", [64, 512], BF16)) for k in range(2)]
        otlo = [ph(nc.sbuf_tensor(f"p3_otl{k}", [64, 512], BF16)) for k in range(2)]

        for k in range(NBH):
            S.step("gpsimd", lambda g, k=k: g.memset(QLA[k][64:128, :], 0.0), w=[QLA[k]])
            S.step("gpsimd", lambda g, k=k: g.memset(QLB[k][0:64, :], 0.0), w=[QLB[k]])

        for hp in range(ND):
            k = hp % NBH
            k2a, k2b, q2a, q2b, qla, qlb = K2A[k], K2B[k], Q2A[k], Q2B[k], QLA[k], QLB[k]
            rA = slice(128 * hp, 128 * hp + 64)
            rB = slice(128 * hp + 64, 128 * hp + 128)
            S.step("ld", lambda s, k2a=k2a, k2b=k2b, q2a=q2a, q2b=q2b, rA=rA, rB=rB: [
                s.dma_start(out=k2a[0:64, :], in_=KTp_h[0][rA, :]),
                s.dma_start(out=k2a[64:128, :], in_=KTp_h[1][rA, :]),
                s.dma_start(out=k2b[0:64, :], in_=KTp_h[1][rB, :]),
                s.dma_start(out=k2b[64:128, :], in_=KTp_h[0][rB, :]),
                s.dma_start(out=q2a[0:64, :], in_=QTp_h[0][rA, :]),
                s.dma_start(out=q2a[64:128, :], in_=QTp_h[0][rA, :]),
                s.dma_start(out=q2b[0:64, :], in_=QTp_h[0][rB, :]),
                s.dma_start(out=q2b[64:128, :], in_=QTp_h[0][rB, :])],
                r=[KTp_h[0], KTp_h[1], QTp_h[0]], w=[k2a, k2b, q2a, q2b], dma="ld", n=8)
            S.step("ld", lambda s, qla=qla, qlb=qlb, rA=rA, rB=rB, hp=hp: [
                s.dma_start(out=qla[0:64, :], in_=QTp_h[1][rA, :]),
                s.dma_start(out=qlb[64:128, :], in_=QTp_h[1][rB, :]),
                s.dma_start(out=vph[:], in_=VTp_h[0][128 * hp:128 * (hp + 1), :]),
                s.dma_start(out=vpl[:], in_=VTp_h[1][128 * hp:128 * (hp + 1), :])],
                r=[QTp_h[1], VTp_h[0], VTp_h[1], qla, qlb, vph, vpl], w=[qla, qlb, vph, vpl], dma="ld", n=4)
            for h2 in range(2) if 'norm' not in p3skip else []:
                ro = slice(0, 64) if h2 == 0 else slice(64, 128)
                qsrc = q2a if h2 == 0 else q2b
                ksrc = k2a if h2 == 0 else k2b
                st4 = stb[k][h2]
                negM = negMb[k][h2]
                for si, srcp in ((0, qsrc), (1, ksrc)):
                    S.step("vector", lambda v, srcp=srcp, ro=ro: v.tensor_tensor(out=sq2[ro, :], in0=srcp[ro, :], in1=srcp[ro, :], op=ALU.mult), r=[srcp, sq2], w=[sq2])
                    for g in range(NG):
                        S.step("tensor", lambda t, g=g, ro=ro: t.matmul(
                            psP[4][0:1, 0:512], ones_bf[ro, :], sq2[ro, 512 * g:512 * (g + 1)], start=True, stop=True),
                            r=[ones_bf, sq2], w=[psP[4]])
                        S.step("vector", lambda v, g=g, si=si, st4=st4: v.tensor_reduce(
                            out=st4[:, 4 * si + g:4 * si + g + 1], in_=psP[4][0:1, 0:512], axis=AX.X, op=ALU.max),
                            r=[psP[4], st4], w=[st4])
                S.step("vector", lambda v, st4=st4: v.tensor_reduce(out=st4[:, 8:9], in_=st4[:, 0:4], axis=AX.X, op=ALU.max), r=[st4], w=[st4])
                S.step("vector", lambda v, st4=st4: v.tensor_reduce(out=st4[:, 9:10], in_=st4[:, 4:8], axis=AX.X, op=ALU.max), r=[st4], w=[st4])
                S.step("vector", lambda v, st4=st4: v.tensor_tensor(out=st4[:, 10:11], in0=st4[:, 8:9], in1=st4[:, 9:10], op=ALU.add), r=[st4], w=[st4])
                S.step("vector", lambda v, st4=st4: v.tensor_scalar(out=st4[:, 11:12], in0=st4[:, 10:11], scalar1=-0.06375, scalar2=None, op0=ALU.mult), r=[st4], w=[st4])
                S.step("tensor", lambda t, st4=st4: t.matmul(psP[4][0:128, 0:1], ones_r[:], st4[:, 11:12], start=True, stop=True), r=[ones_r, st4], w=[psP[4]])
                S.step("vector", lambda v, negM=negM: v.tensor_copy(negM[:], psP[4][0:128, 0:1]), r=[psP[4]], w=[negM])
            for kc in range(NT) if 'vch' not in p3skip else []:
                S.step("tensor", lambda t, kc=kc: [
                    t.transpose(psB16[:, 0:128], vph[:, 128 * kc:128 * (kc + 1)], ebf[:]),
                    t.transpose(psB16[:, 128:256], vpl[:, 128 * kc:128 * (kc + 1)], ebf[:])],
                    r=[vph, vpl, ebf], w=[psB16])
                def fev(v, kc=kc):
                    ops = []
                    for idx, (h2, p) in enumerate(((0, 0), (1, 0), (0, 1), (1, 1))):
                        vt = vchb[h2][p][kc]
                        ops.append(v.tensor_copy(vt[:, 0:64], psB16[:, 128 * p + 64 * h2:128 * p + 64 * (h2 + 1)]))
                        ops.append(v.tensor_copy(vt[:, 64:65], ones_bf[:] if p == 0 else zeros_bf[:]))
                    return ops
                vts = [vchb[h2][p][kc] for h2 in range(2) for p in range(2)]
                S.step("vector", fev, r=[psB16, ones_bf, zeros_bf] + vts, w=vts)
            for qt in range(NG) if 'attn' not in p3skip else []:
                nkc = 4 * (qt + 1)

                def logits(kc, qt=qt, k=k):
                    for h2 in range(2):
                        psX = psP[2 * (kc % 2) + h2]
                        ksrc = (K2A, K2B)[h2][k]
                        qsrc = (Q2A, Q2B)[h2][k]
                        qlo = (QLA, QLB)[h2][k]
                        def fmm(t, kc=kc, psX=psX, ksrc=ksrc, qsrc=qsrc, qlo=qlo):
                            return [
                                t.matmul(psX[:], ksrc[:, 128 * kc:128 * (kc + 1)], qsrc[:, 512 * qt:512 * (qt + 1)], start=True, stop=False),
                                t.matmul(psX[:], ksrc[:, 128 * kc:128 * (kc + 1)], qlo[:, 512 * qt:512 * (qt + 1)], start=False, stop=True)]
                        S.step("tensor", fmm, r=[ksrc, qsrc, qlo], w=[psX])

                def mask(kc, qt=qt):
                    if kc >= 4 * qt:
                        rel = kc - 4 * qt
                        wmask = 128 * (rel + 1)
                        for h2 in range(2):
                            psX = psP[2 * (kc % 2) + h2]
                            S.step("vector", lambda v, rel=rel, psX=psX, wmask=wmask: v.tensor_tensor(
                                out=psX[:, 0:wmask], in0=psX[:, 0:wmask], in1=masks[rel][:, 0:wmask], op=ALU.add), r=[psX, masks[rel]], w=[psX])

                def epi(kc, qt=qt, nkc=nkc, k=k):
                    for h2 in range(2):
                        psX = psP[2 * (kc % 2) + h2]
                        psO = psP[5 + h2]
                        pf = ptf[2 * (kc % 2) + h2]
                        phh = pth[2 * (kc % 2) + h2]
                        pll = ptl[2 * (kc % 2) + h2]
                        negM = negMb[k][h2]
                        S.step("scalar", lambda s, psX=psX, pf=pf, negM=negM: s.activation(out=pf[:], in_=psX[:], func=AF.Exp, scale=0.125, bias=negM[:]), r=[psX, negM], w=[pf])
                        if h2 == 0:
                            S.step("scalar", lambda s, pf=pf, phh=phh: s.copy(phh[:], pf[:]), r=[pf, phh], w=[phh])
                            S.step("vector", lambda v, pf=pf, phh=phh, pll=pll: v.tensor_tensor(out=pll[:], in0=pf[:], in1=phh[:], op=ALU.subtract), r=[pf, phh, pll], w=[pll])
                        else:
                            S.step("vector", lambda v, pf=pf, phh=phh: v.tensor_copy(phh[:], pf[:]), r=[pf, phh], w=[phh])
                            S.step("vector", lambda g, pf=pf, phh=phh, pll=pll: g.tensor_tensor(out=pll[:], in0=pf[:], in1=phh[:], op=ALU.subtract), r=[pf, phh, pll], w=[pll])
                        vh_ = vchb[h2][0][kc]
                        vl_ = vchb[h2][1][kc]
                        def fav(t, kc=kc, nkc=nkc, vh_=vh_, vl_=vl_, phh=phh, pll=pll, psO=psO):
                            return [
                                t.matmul(psO[0:65, :], vh_[:], phh[:], start=(kc == 0), stop=False),
                                t.matmul(psO[0:65, :], vh_[:], pll[:], start=False, stop=False),
                                t.matmul(psO[0:65, :], vl_[:], phh[:], start=False, stop=(kc == nkc - 1))]
                        S.step("tensor", fav, r=[vh_, vl_, phh, pll], w=[psO])

                logits(0)
                mask(0)
                for kc in range(1, nkc):
                    logits(kc)
                    mask(kc)
                    epi(kc - 1)
                epi(nkc - 1)
                for h2 in range(2):
                    psO = psP[5 + h2]
                    rs = rsb[h2]
                    rr = rrb[h2]
                    ot = otm[h2]
                    S.step("vector", lambda v, rs=rs, psO=psO: v.reciprocal(rs[:], psO[64:65, :]), r=[psO, rs], w=[rs])
                    S.step("tensor", lambda t, rs=rs: t.matmul(psP[4][0:64, 0:512], ones_r[:, 0:64], rs[:], start=True, stop=True), r=[ones_r, rs], w=[psP[4]])
                    S.step("vector", lambda v, rr=rr: v.tensor_copy(rr[:], psP[4][0:64, 0:512]), r=[psP[4], rr], w=[rr])
                    S.step("vector", lambda v, rr=rr, ot=ot, psO=psO: v.tensor_tensor(out=ot[:], in0=psO[0:64, :], in1=rr[:], op=ALU.mult), r=[psO, rr, ot], w=[ot])
                    ohi, olo = othi[h2], otlo[h2]
                    S.step("vector", lambda v, ot=ot, ohi=ohi: v.tensor_copy(ohi[:], ot[:]), r=[ot, ohi], w=[ohi])
                    S.step("vector", lambda v, ot=ot, ohi=ohi, olo=olo: v.tensor_tensor(out=olo[:], in0=ot[:], in1=ohi[:], op=ALU.subtract), r=[ot, ohi, olo], w=[olo])
                    po = 128 * hp + 64 * h2
                    S.step("st", lambda s, ohi=ohi, olo=olo, po=po, qt=qt: [
                        s.dma_start(out=OCTH_h[po:po + 64, 512 * qt:512 * (qt + 1)], in_=ohi[:]),
                        s.dma_start(out=OCTL_h[po:po + 64, 512 * qt:512 * (qt + 1)], in_=olo[:])],
                        r=[ohi, olo], w=[OCTH_h, OCTL_h], dma="st", n=2)

    with Phase() as ph:
        shb = [[ph(nc.sbuf_tensor(f"p4_sh{k}_{i}", [128, 128], BF16)) for i in range(ND)] for k in range(2)]
        slb = [[ph(nc.sbuf_tensor(f"p4_sl{k}_{i}", [128, 128], BF16)) for i in range(ND)] for k in range(2)]
        ocf = [ph(nc.sbuf_tensor(f"p4_oh_{i}", [128, T], BF16)) for i in range(ND)]
        olf = [ph(nc.sbuf_tensor(f"p4_ol_{i}", [128, T], BF16)) for i in range(ND)]
        evb = [ph(nc.sbuf_tensor(f"p4_ev{k}", [128, 512], F32)) for k in range(3)]
        woh, wol = wTp["o"]
        S.step("ld", lambda s: [s.dma_start(out=ocf[i][:], in_=OCTH_h[128 * i:128 * (i + 1), :]) for i in range(ND)]
               + [s.dma_start(out=olf[i][:], in_=OCTL_h[128 * i:128 * (i + 1), :]) for i in range(ND)],
               r=[OCTH_h, OCTL_h], w=ocf + olf, dma="ld", n=2 * ND)
        for of in range(ND):
            sh, sl = shb[of % 2], slb[of % 2]
            S.step("ld", lambda s, of=of, sh=sh, sl=sl: [
                s.dma_start(out=sh[i][:], in_=woh[128 * i:128 * (i + 1), 128 * of:128 * (of + 1)])
                for i in range(ND)] + [
                s.dma_start(out=sl[i][:], in_=wol[128 * i:128 * (i + 1), 128 * of:128 * (of + 1)])
                for i in range(ND)], r=[woh, wol], w=sh + sl, dma="ld", n=2 * ND)
            for tg in range(NG):
                ps = psP[tg % 2]
                ev = evb[tg % 3]
                def fmm(t, tg=tg, sh=sh, sl=sl, ps=ps):
                    mm = []
                    for i in range(ND):
                        sli = slice(512 * tg, 512 * (tg + 1))
                        mm.append(t.matmul(ps[:], sh[i][:], ocf[i][:, sli], start=(i == 0), stop=False))
                        mm.append(t.matmul(ps[:], sh[i][:], olf[i][:, sli], start=False, stop=False))
                        mm.append(t.matmul(ps[:], sl[i][:], ocf[i][:, sli], start=False, stop=(i == ND - 1)))
                    return mm
                S.step("tensor", fmm, r=sh + sl + ocf + olf, w=[ps])
                S.step("vector", lambda v, ps=ps, ev=ev: v.tensor_copy(ev[:], ps[:]), r=[ps], w=[ev])
                S.step("st", lambda s, of=of, tg=tg, ev=ev: s.dma_start(
                    out=ATT_h[128 * of:128 * (of + 1), 512 * tg:512 * (tg + 1)], in_=ev[:]), r=[ev], w=[ATT_h], dma="st")

        statb = [[ph(nc.sbuf_tensor(f"p5_st{k}_{i}", [128, 128], BF16)) for i in range(ND)] for k in range(2)]
        hb = [ph(nc.sbuf_tensor(f"p5_h{k}", [128, T], F32)) for k in range(2)]
        yb = [ph(nc.sbuf_tensor(f"p5_y{k}", [128, T], F32)) for k in range(2)]
        gb = [ph(nc.sbuf_tensor(f"p5_g{k}", [128, T], F32)) for k in range(2)]
        gabb = [ph(nc.sbuf_tensor("p5_gab0", [128, T], F32))] * 2
        M1 = ph(nc.sbuf_tensor("p5_M1", [128, T], F32))
        cwb = [ph(nc.sbuf_tensor(f"p5_cw{k}", [128, 3], F32)) for k in range(2)]
        cbb = [ph(nc.sbuf_tensor(f"p5_cb{k}", [128, 1], F32)) for k in range(2)]
        R1REP = ph(nc.sbuf_tensor("p5_R1REP", [128, T], F32))
        r1row_sb = ph(nc.sbuf_tensor("p5_r1row", [1, T], F32))
        S.step("ld", lambda s: s.dma_start(out=r1row_sb[:], in_=R1ROW[:]), r=[R1ROW], w=[r1row_sb], dma="ld")
        for g in range(NG):
            S.step("tensor", lambda t, g=g: t.matmul(psP[4][:], ones_r[:], r1row_sb[0:1, 512 * g:512 * (g + 1)], start=True, stop=True), r=[ones_r, r1row_sb], w=[psP[4]])
            S.step("vector", lambda v, g=g: v.tensor_copy(R1REP[:, 512 * g:512 * (g + 1)], psP[4][:]), r=[psP[4], R1REP], w=[R1REP])
        for i in range(NF):
            k = i % 2
            stat, hs, ys, gs, gab = statb[k], hb[k], yb[k], gb[k], gabb[k]
            cwt, cbt = cwb[k], cbb[k]
            S.step("ld", lambda s, i=i, stat=stat, cwt=cwt, cbt=cbt: [
                s.dma_start(out=stat[j][:], in_=w1qT_h[128 * j:128 * (j + 1), 128 * i:128 * (i + 1)])
                for j in range(ND)] + [
                s.dma_start(out=cwt[:], in_=cw_h[128 * i:128 * (i + 1), :]),
                s.dma_start(out=cbt[:], in_=cb_h[128 * i:128 * (i + 1)].rearrange("(p o) -> p o", o=1))],
                r=[w1qT_h, cw_h, cb_h], w=stat + [cwt, cbt], dma="ld", n=ND + 2)
            for tg in range(NG):
                ps = psP[tg % 2]
                def fmm(t, tg=tg, stat=stat, ps=ps):
                    return [t.matmul(ps[:], stat[j][:], aT[j][:, 512 * tg:512 * (tg + 1)],
                                     start=(j == 0), stop=(j == ND - 1)) for j in range(ND)]
                S.step("tensor", fmm, r=stat + aT, w=[ps])
                S.step("vector", lambda v, tg=tg, ps=ps, hs=hs: v.scalar_tensor_tensor(
                    out=hs[:, 512 * tg:512 * (tg + 1)], in0=ps[:], scalar=m1bc[:], in1=R1REP[:, 512 * tg:512 * (tg + 1)],
                    op0=ALU.mult, op1=ALU.mult), r=[ps, m1bc, R1REP, hs], w=[hs])
            S.step("vector", lambda v, hs=hs, ys=ys, cwt=cwt, cbt=cbt: v.tensor_scalar(out=ys[:], in0=hs[:], scalar1=cwt[:, 1:2], scalar2=cbt[:], op0=ALU.mult, op1=ALU.add), r=[hs, cwt, cbt], w=[ys])
            S.step("vector", lambda v, hs=hs, ys=ys, cwt=cwt: v.scalar_tensor_tensor(out=ys[:, 1:T], in0=hs[:, 0:T - 1], scalar=cwt[:, 0:1], in1=ys[:, 1:T], op0=ALU.mult, op1=ALU.add), r=[hs, ys, cwt], w=[ys])
            S.step("vector", lambda v, hs=hs, ys=ys, cwt=cwt: v.scalar_tensor_tensor(out=ys[:, 0:T - 1], in0=hs[:, 1:T], scalar=cwt[:, 2:3], in1=ys[:, 0:T - 1], op0=ALU.mult, op1=ALU.add), r=[hs, ys, cwt], w=[ys])
            S.step("scalar", lambda s, ys=ys, gs=gs: s.activation(out=gs[:], in_=ys[:], func=AF.Silu), r=[ys], w=[gs])
            if i == 0:
                S.step("vector", lambda v, gs=gs: v.scalar_tensor_tensor(out=M1[:], in0=gs[:], scalar=-1.0, in1=gs[:], op0=ALU.mult, op1=ALU.max), r=[gs], w=[M1])
            else:
                S.step("vector", lambda v, gs=gs, gab=gab: v.scalar_tensor_tensor(out=gab[:], in0=gs[:], scalar=-1.0, in1=gs[:], op0=ALU.mult, op1=ALU.max), r=[gs], w=[gab])
                S.step("vector", lambda v, gab=gab: v.tensor_tensor(out=M1[:], in0=M1[:], in1=gab[:], op=ALU.max), r=[M1, gab], w=[M1])
            S.step("st", lambda s, i=i, gs=gs: s.dma_start(out=G_h[128 * i:128 * (i + 1), :], in_=gs[:]), r=[gs], w=[G_h], dma="st")

        mcol = ph(nc.sbuf_tensor("p5_mcol", [128, 32], F32))
        rowp = ph(nc.sbuf_tensor("p5_rowp", [32, 128], F32))
        for j in range(NT):
            S.step("tensor", lambda t, j=j: t.transpose(psP[5][:, 0:128], M1[:, 128 * j:128 * (j + 1)], eye[:]), r=[M1, eye], w=[psP[5]])
            S.step("vector", lambda v, j=j: v.tensor_reduce(out=mcol[:, j:j + 1], in_=psP[5][:, 0:128], axis=AX.X, op=ALU.max), r=[psP[5], mcol], w=[mcol])
        S.step("vector", lambda v: v.tensor_scalar(out=mcol[:, 0:16], in0=mcol[:, 0:16], scalar1=1e-5, scalar2=None, op0=ALU.max), r=[mcol], w=[mcol])
        S.step("vector", lambda v: v.reciprocal(mcol[:, 16:32], mcol[:, 0:16]), r=[mcol], w=[mcol])
        S.step("vector", lambda v: v.tensor_scalar(out=mcol[:, 16:32], in0=mcol[:, 16:32], scalar1=127.0, scalar2=None, op0=ALU.mult), r=[mcol], w=[mcol])
        S.step("vector", lambda v: v.reciprocal(mcol[:, 0:16], mcol[:, 16:32]), r=[mcol], w=[mcol])
        S.step("tensor", lambda t: t.transpose(psP[5][0:32, 0:128], mcol[:, 0:32], eye[:]), r=[mcol, eye], w=[psP[5]])
        S.step("vector", lambda v: v.tensor_copy(rowp[:], psP[5][0:32, 0:128]), r=[psP[5]], w=[rowp])
        S.step("st", lambda s: [s.dma_start(out=S2ROW[0:1, 128 * j:128 * (j + 1)], in_=rowp[16 + j:17 + j, :]) for j in range(16)], r=[rowp], w=[S2ROW], dma="st", n=16)
        S.step("st", lambda s: [s.dma_start(out=RS2ROW[0:1, 128 * j:128 * (j + 1)], in_=rowp[j:j + 1, :]) for j in range(16)], r=[rowp], w=[RS2ROW], dma="st", n=16)

    with Phase() as ph:
        w2sb = [ph(nc.sbuf_tensor(f"p5c_w2_{i}", [128, D], BF16)) for i in range(NF)]
        gqb = [[ph(nc.sbuf_tensor(f"p5c_gq{kk}_{i}", [128, 512], BF16)) for i in range(NF)] for kk in range(2)]
        glb = [ph(nc.sbuf_tensor(f"p5c_gl{k}", [128, 512], F32)) for k in range(2)]
        ubb = [ph(nc.sbuf_tensor(f"p5c_u{k}", [128, 512], F32)) for k in range(2)]
        cib = [ph(nc.sbuf_tensor(f"p5c_ci{k}", [128, 512], I32)) for k in range(2)]
        cm1b = [ph(nc.sbuf_tensor(f"p5c_cm1{k}", [128, 512], F32)) for k in range(2)]
        dbb = [ph(nc.sbuf_tensor(f"p5c_d{k}", [128, 512], F32)) for k in range(2)]
        evb = [ph(nc.sbuf_tensor(f"p5c_ev{k}", [128, 512], F32)) for k in range(3)]
        S2repb = [ph(nc.sbuf_tensor(f"p5c_S2{kk}", [128, 512], F32)) for kk in range(2)]
        RS2repb = [ph(nc.sbuf_tensor(f"p5c_RS2{kk}", [128, 512], F32)) for kk in range(2)]
        S.step("ld", lambda s: [s.dma_start(out=w2sb[i][:], in_=w2qT_h[128 * i:128 * (i + 1), :]) for i in range(NF)],
               r=[w2qT_h], w=w2sb, dma="ld", n=NF)
        srowb = [ph(nc.sbuf_tensor("p5c_srow0", [1, 1024], F32))] * 2
        for tb in range(NG):
            gq = gqb[tb % 2]
            S2rep, RS2rep = S2repb[tb % 2], RS2repb[tb % 2]
            srow_l = srowb[tb % 2]
            S.step("ld", lambda s, tb=tb: [
                s.dma_start(out=srow_l[0:1, 0:512], in_=S2ROW[0:1, 512 * tb:512 * (tb + 1)]),
                s.dma_start(out=srow_l[0:1, 512:1024], in_=RS2ROW[0:1, 512 * tb:512 * (tb + 1)])],
                r=[S2ROW, RS2ROW], w=[srow_l], dma="ld", n=2)
            S.step("tensor", lambda t: t.matmul(psP[4][:], ones_r[:], srow_l[0:1, 0:512], start=True, stop=True), r=[ones_r, srow_l], w=[psP[4]])
            S.step("vector", lambda v, S2rep=S2rep: v.tensor_copy(S2rep[:], psP[4][:]), r=[psP[4]], w=[S2rep])
            S.step("tensor", lambda t: t.matmul(psP[4][:], ones_r[:], srow_l[0:1, 512:1024], start=True, stop=True), r=[ones_r, srow_l], w=[psP[4]])
            S.step("vector", lambda v, RS2rep=RS2rep: v.tensor_copy(RS2rep[:], psP[4][:]), r=[psP[4]], w=[RS2rep])
            for i in range(NF):
                k = i % 2
                gl, ub, ci, cm, db = glb[k], ubb[k], cib[k], cm1b[k], dbb[k]
                S.step("ld", lambda s, i=i, tb=tb, gl=gl: s.dma_start(out=gl[:], in_=G_h[128 * i:128 * (i + 1), 512 * tb:512 * (tb + 1)]), r=[G_h], w=[gl], dma="ld")
                S.step("vector", lambda g, gl=gl, ub=ub, S2rep=S2rep: g.tensor_tensor(out=ub[:], in0=gl[:], in1=S2rep[:], op=ALU.mult), r=[gl, S2rep], w=[ub])
                S.step("vector", lambda v, ub=ub, ci=ci: v.tensor_copy(ci[:], ub[:]), r=[ub], w=[ci])
                S.step("vector", lambda g, ci=ci, cm=cm: g.tensor_scalar(out=cm[:], in0=ci[:], scalar1=-1.0, scalar2=None, op0=ALU.add), r=[ci], w=[cm])
                S.step("vector", lambda v, ub=ub, cm=cm, db=db: v.scalar_tensor_tensor(out=db[:], in0=ub[:], scalar=-1.0, in1=cm[:], op0=ALU.add, op1=ALU.subtract), r=[ub, cm], w=[db])
                S.step("vector", lambda v, db=db, cm=cm, ub=ub: v.scalar_tensor_tensor(out=ub[:], in0=db[:], scalar=0.5, in1=cm[:], op0=ALU.is_ge, op1=ALU.add), r=[db, cm], w=[ub])
                S.step("vector", lambda v, db=db, ub=ub, i=i, gq=gq: v.scalar_tensor_tensor(out=gq[i][:], in0=db[:], scalar=-0.5, in1=ub[:], op0=ALU.is_gt, op1=ALU.add), r=[db, ub], w=[gq[i]])
            for of in range(ND):
                ps = psP[of % 2]
                ev = evb[of % 3]
                def fmm(t, of=of, ps=ps, gq=gq):
                    return [t.matmul(ps[:], w2sb[i][:, 128 * of:128 * (of + 1)], gq[i][:], start=(i == 0), stop=(i == NF - 1))
                            for i in range(NF)]
                S.step("tensor", fmm, r=w2sb + gq, w=[ps])
                S.step("vector", lambda v, ps=ps, ev=ev, RS2rep=RS2rep: v.scalar_tensor_tensor(
                    out=ev[:], in0=ps[:], scalar=m2bc[:], in1=RS2rep[:], op0=ALU.mult, op1=ALU.mult), r=[ps, m2bc, RS2rep], w=[ev])
                S.step("st", lambda s, of=of, tb=tb, ev=ev: s.dma_start(
                    out=FFT_h[128 * of:128 * (of + 1), 512 * tb:512 * (tb + 1)], in_=ev[:]), r=[ev], w=[FFT_h], dma="st")

    with Phase() as ph:
        xb = [ph(nc.sbuf_tensor(f"p6_x{k}", [128, D], F32)) for k in range(2)]
        ab = [ph(nc.sbuf_tensor(f"p6_a{k}", [128, D], F32)) for k in range(3)]
        fb = [ph(nc.sbuf_tensor(f"p6_f{k}", [128, D], F32)) for k in range(3)]
        sb = [ph(nc.sbuf_tensor(f"p6_s{k}", [128, D], F32)) for k in range(3)]
        ob = [ph(nc.sbuf_tensor(f"p6_o{k}", [128, D], F32)) for k in range(2)]
        ATT_r = ATT_h.rearrange("(c p) t -> p c t", p=128)
        FFT_r = FFT_h.rearrange("(c p) t -> p c t", p=128)
        for tt in range(NT):
            k = tt % 2
            xs, os_ = xb[k], ob[k]
            av, fv, sv = ab[tt % 3], fb[tt % 3], sb[tt % 3]
            ps0, ps1 = psP[2 * k], psP[2 * k + 1]
            S.step("ld", lambda s, tt=tt, xs=xs: s.dma_start(out=xs[:], in_=x_h[128 * tt:128 * (tt + 1), :]), r=[x_h], w=[xs], dma="ld")
            S.step("ld", lambda s, tt=tt, av=av, fv=fv: [
                s.dma_start(out=av[:].rearrange("p (c t) -> p c t", c=ND), in_=ATT_r[:, :, 128 * tt:128 * (tt + 1)]),
                s.dma_start(out=fv[:].rearrange("p (c t) -> p c t", c=ND), in_=FFT_r[:, :, 128 * tt:128 * (tt + 1)])],
                r=[ATT_h, FFT_h], w=[av, fv], dma="ld", n=2)
            S.step("vector", lambda v, av=av, fv=fv, sv=sv: v.tensor_tensor(out=sv[:], in0=av[:], in1=fv[:], op=ALU.add), r=[av, fv], w=[sv])
            for dc in range(ND):
                ps = ps0 if dc < 4 else ps1
                S.step("tensor", lambda t, dc=dc, sv=sv, ps=ps: t.transpose(ps[:, 128 * (dc % 4):128 * (dc % 4 + 1)], sv[:, 128 * dc:128 * (dc + 1)], eye[:]), r=[sv, eye], w=[ps])
            S.step("vector", lambda v, xs=xs, os_=os_, ps0=ps0: v.tensor_tensor(out=os_[:, 0:512], in0=ps0[:], in1=xs[:, 0:512], op=ALU.add), r=[ps0, xs], w=[os_])
            S.step("vector", lambda v, xs=xs, os_=os_, ps1=ps1: v.tensor_tensor(out=os_[:, 512:1024], in0=ps1[:], in1=xs[:, 512:1024], op=ALU.add), r=[ps1, xs, os_], w=[os_])
            S.step("st", lambda s, tt=tt, os_=os_: s.dma_start(out=out_h[128 * tt:128 * (tt + 1), :], in_=os_[:]), r=[os_], w=[out_h], dma="st")

    top.close()
    return nc


def kernel(L0_attn_x, L0_attn_Wq, L0_attn_Wk, L0_attn_Wv, L0_attn_Wo,
           L0_ffn_w1, L0_ffn_conv_w, L0_ffn_conv_b, L0_ffn_w2):
    x = np.ascontiguousarray(np.asarray(L0_attn_x, dtype=np.float32))
    ws = {
        "wq": np.ascontiguousarray(np.asarray(L0_attn_Wq, np.float32)),
        "wk": np.ascontiguousarray(np.asarray(L0_attn_Wk, np.float32)),
        "wv": np.ascontiguousarray(np.asarray(L0_attn_Wv, np.float32)),
        "wo": np.ascontiguousarray(np.asarray(L0_attn_Wo, np.float32)),
        "w1": np.ascontiguousarray(np.asarray(L0_ffn_w1, np.float32)),
        "cw": np.ascontiguousarray(np.asarray(L0_ffn_conv_w, np.float32).reshape(DFF, 3)),
        "cb": np.ascontiguousarray(np.asarray(L0_ffn_conv_b, np.float32)),
        "w2": np.ascontiguousarray(np.asarray(L0_ffn_w2, np.float32)),
    }
    nc = build_program()
    in_maps = [dict(ws, x=np.ascontiguousarray(x[c])) for c in range(8)]
    res = run_bass_kernel_spmd(nc, in_maps, core_ids=list(range(8)))
    return np.stack([np.asarray(r["out"], np.float32) for r in res.results], axis=0)


# version 25 (final)
# speedup vs baseline: 1.1840x; 1.1840x over previous
import os
import numpy as np
from contextlib import ExitStack

import concourse.bass as bass
import concourse.mybir as mybir
from concourse.bass_utils import run_bass_kernel_spmd

F32 = mybir.dt.float32
BF16 = mybir.dt.bfloat16
I32 = mybir.dt.int32
AF = mybir.ActivationFunctionType
ALU = mybir.AluOpType
AX = mybir.AxisListType

T, D, H, DH, DFF = 2048, 1024, 16, 64, 4096
NT, NG, ND, NF = T // 128, T // 512, D // 128, DFF // 128
NEG = -1.0e30
ENGS = ("tensor", "vector", "scalar", "gpsimd", "sync")


class Sched:

    def __init__(self, nc, sems):
        self.nc = nc
        self.sems = sems
        self.cnt = {k: 0 for k in sems}
        self.w = {}
        self.r = {}
        self.clock = {e: {s: 0 for s in sems} for e in ENGS}
        self.pool = {"ld": [k for k in sems if k.startswith("ld")],
                     "st": [k for k in sems if k.startswith("st")]}
        self.pidx = {"ld": 0, "st": 0}
        self.progs = {e: [] for e in ENGS}

    def step(self, eng, emit, r=(), w=(), dma=None, n=1):
        issue_eng = {"ld": "sync", "st": "gpsimd"}.get(dma, eng)
        dsem = None
        if dma:
            pool = self.pool[dma]
            dsem = pool[self.pidx[dma] % len(pool)]
            self.pidx[dma] += 1
        deps = {}

        def add(sem, c):
            if sem == "tensor" and issue_eng == "tensor":
                return
            if c > deps.get(sem, 0):
                deps[sem] = c

        for b in r:
            d = self.w.get(id(b))
            if d:
                add(*d)
        for b in w:
            d = self.w.get(id(b))
            if d:
                add(*d)
            for sem, c in self.r.get(id(b), {}).items():
                add(sem, c)
        if dsem is not None and self.cnt[dsem] > deps.get(dsem, 0):
            deps[dsem] = self.cnt[dsem]
        clk = self.clock[issue_eng]
        waits = [(s, c) for s, c in deps.items() if clk[s] < c]
        for s, c in waits:
            clk[s] = c
        sem = dsem if dma else eng
        inc = 16 * n if dma else 1
        self.cnt[sem] += inc
        nexp = n

        def op(e, waits=waits, emit=emit, sem=sem, dma=dma, nexp=nexp):
            for s, c in waits:
                e.wait_ge(self.sems[s], c)
            insts = emit(e)
            if not isinstance(insts, (list, tuple)):
                insts = [insts]
            if dma:
                assert len(insts) == nexp, f"dma step declared n={nexp}, emitted {len(insts)}"
                for i in insts:
                    i.then_inc(self.sems[sem], 16)
            else:
                insts[-1].then_inc(self.sems[sem], 1)
        self.progs[issue_eng].append(op)

        end = self.cnt[sem]
        for b in w:
            self.w[id(b)] = (sem, end)
            self.r[id(b)] = {}
        for b in r:
            m = self.r.setdefault(id(b), {})
            m[sem] = max(m.get(sem, 0), end)

    def drain(self):
        def op(e):
            for s in self.sems:
                if self.cnt[s] > self.clock["sync"][s]:
                    e.wait_ge(self.sems[s], self.cnt[s])
                    self.clock["sync"][s] = self.cnt[s]
            e.nop()
        self.progs["sync"].append(op)

    def flush(self, blk):
        for engname in ENGS:
            ops = self.progs[engname]
            if not ops:
                continue

            def body(e, ops=tuple(ops)):
                for op in ops:
                    op(e)
            getattr(blk, engname)(body)
            self.progs[engname] = []


def build_program(debug=False, upto=99):
    p3skip = set(os.environ.get('K_P3SKIP', '').split(','))
    nc = bass.Bass("TRN2", target_bir_lowering=False)
    SK = "ExternalOutput" if debug else "Internal"

    x_h = nc.dram_tensor("x", [T, D], F32, kind="ExternalInput").ap()
    wq_h = nc.dram_tensor("wq", [D, D], F32, kind="ExternalInput").ap()
    wk_h = nc.dram_tensor("wk", [D, D], F32, kind="ExternalInput").ap()
    wv_h = nc.dram_tensor("wv", [D, D], F32, kind="ExternalInput").ap()
    wo_h = nc.dram_tensor("wo", [D, D], F32, kind="ExternalInput").ap()
    w1_h = nc.dram_tensor("w1", [DFF, D], F32, kind="ExternalInput").ap()
    cw_h = nc.dram_tensor("cw", [DFF, 3], F32, kind="ExternalInput").ap()
    cb_h = nc.dram_tensor("cb", [DFF], F32, kind="ExternalInput").ap()
    w2_h = nc.dram_tensor("w2", [D, DFF], F32, kind="ExternalInput").ap()
    out_h = nc.dram_tensor("out", [T, D], F32, kind="ExternalOutput").ap()

    wTp = {}
    for nm in "qkvo":
        wTp[nm] = (nc.dram_tensor(f"w{nm}Th", [D, D], BF16, kind=SK).ap(),
                   nc.dram_tensor(f"w{nm}Tl", [D, D], BF16, kind=SK).ap())
    w1qT_h = nc.dram_tensor("w1qT", [D, DFF], BF16, kind=SK).ap()
    w2qT_h = nc.dram_tensor("w2qT", [DFF, D], BF16, kind=SK).ap()
    QTp_h = (nc.dram_tensor("QTh", [D, T], BF16, kind=SK).ap(),
             nc.dram_tensor("QTl", [D, T], BF16, kind=SK).ap())
    KTp_h = (nc.dram_tensor("KTh", [D, T], BF16, kind=SK).ap(),
             nc.dram_tensor("KTl", [D, T], BF16, kind=SK).ap())
    VTp_h = (nc.dram_tensor("VTh", [D, T], BF16, kind=SK).ap(),
             nc.dram_tensor("VTl", [D, T], BF16, kind=SK).ap())
    ATT_h = nc.dram_tensor("ATT", [D, T], F32, kind=SK).ap()
    G_h = nc.dram_tensor("G", [DFF, T], F32, kind=SK).ap()
    OCTH_h = nc.dram_tensor("OCTH", [D, T], BF16, kind=SK).ap()
    OCTL_h = nc.dram_tensor("OCTL", [D, T], BF16, kind=SK).ap()
    R1ROW = nc.dram_tensor("R1ROW", [1, T], F32, kind=SK).ap()
    S2ROW = nc.dram_tensor("S2ROW", [1, T], F32, kind=SK).ap()
    RS2ROW = nc.dram_tensor("RS2ROW", [1, T], F32, kind=SK).ap()
    FFT_h = nc.dram_tensor("FFT", [D, T], F32, kind=SK).ap()

    top = ExitStack()
    e = top.enter_context

    eye = e(nc.sbuf_tensor("eye", [128, 128], F32))
    ebf = e(nc.sbuf_tensor("ebf", [128, 128], BF16))
    ones_c = e(nc.sbuf_tensor("ones_c", [128, 1], F32))
    ones_r = e(nc.sbuf_tensor("ones_r", [1, 128], F32))
    masks = [e(nc.sbuf_tensor(f"mask{r}", [128, 512], F32)) for r in range(4)]
    m1bc = e(nc.sbuf_tensor("m1bc", [128, 1], F32))
    m2bc = e(nc.sbuf_tensor("m2bc", [128, 1], F32))
    sw1bc = e(nc.sbuf_tensor("sw1bc", [128, 1], F32))
    sw2bc = e(nc.sbuf_tensor("sw2bc", [128, 1], F32))
    aT = [e(nc.sbuf_tensor(f"aT{i}", [128, T], BF16)) for i in range(ND)]
    ones_bf = e(nc.sbuf_tensor("ones_bf", [128, 1], BF16))
    zeros_bf = e(nc.sbuf_tensor("zeros_bf", [128, 1], BF16))


    stkI = ExitStack()
    iota_p = stkI.enter_context(nc.sbuf_tensor("iota_p", [128, 512], F32))
    iota_c = stkI.enter_context(nc.sbuf_tensor("iota_c", [128, 512], F32))

    psP = [e(nc.psum_tensor(f"psP{i}", [128, 512], F32)) for i in range(7)]
    psB16 = e(nc.psum_tensor("psB16", [128, 1024], BF16))

    semnames = (["tensor", "vector", "scalar", "gpsimd"]
                + [f"ld{i}" for i in range(16)] + [f"st{i}" for i in range(8)])
    sems = {n: e(nc.semaphore(f"sem_{n}")) for n in semnames}
    S = Sched(nc, sems)

    phase_no = [0]

    class Phase:
        def __init__(self):
            self.stack = ExitStack()
            phase_no[0] += 1
            self.skip = phase_no[0] > upto

        def __enter__(self):
            self.blk = nc.Block()
            self.blk.__enter__()
            return self.stack.enter_context

        def __exit__(self, *a):
            if self.skip:
                for eng in ENGS:
                    S.progs[eng] = []
            else:
                S.drain()
                S.flush(self.blk)
            self.blk.__exit__(*a)
            self.stack.close()
            return False

    with nc.Block() as blk0:
        def _clr(g):
            for s in sems.values():
                g.sem_clear(s)
        blk0.gpsimd(_clr)

    with Phase() as ph:
        S.step("gpsimd", lambda g: [
            g.iota(iota_p[:], [[0, 512]], channel_multiplier=1, allow_small_or_imprecise_dtypes=True),
            g.iota(iota_c[:], [[1, 512]], channel_multiplier=0, allow_small_or_imprecise_dtypes=True)],
            w=[iota_p, iota_c])
        S.step("vector", lambda v: v.tensor_tensor(eye[:], iota_p[:, 0:128], iota_c[:, 0:128], op=ALU.is_equal),
               r=[iota_p, iota_c], w=[eye])
        S.step("vector", lambda v: v.tensor_copy(ebf[:], eye[:]), r=[eye], w=[ebf])
        S.step("gpsimd", lambda g: g.memset(ones_c[:], 1.0), w=[ones_c])
        S.step("gpsimd", lambda g: g.memset(ones_r[:], 1.0), w=[ones_r])
        S.step("gpsimd", lambda g: g.memset(ones_bf[:], 1.0), w=[ones_bf])
        S.step("gpsimd", lambda g: g.memset(zeros_bf[:], 0.0), w=[zeros_bf])
        for rl in range(4):
            S.step("vector", lambda v, rl=rl: v.scalar_tensor_tensor(
                out=masks[rl][:], in0=iota_p[:], scalar=float(128 * rl), in1=iota_c[:],
                op0=ALU.add, op1=ALU.is_gt), r=[iota_p, iota_c], w=[masks[rl]])
            S.step("vector", lambda v, rl=rl: v.tensor_scalar(
                out=masks[rl][:], in0=masks[rl][:], scalar1=NEG, scalar2=None, op0=ALU.mult),
                r=[masks[rl]], w=[masks[rl]])

    stkI.close()

    with Phase() as ph:
        wsb = [ph(nc.sbuf_tensor(f"p1_w{k}", [128, D], F32)) for k in range(2)]
        whb = [ph(nc.sbuf_tensor(f"p1_wh{k}", [128, D], BF16)) for k in range(2)]
        wlb = [ph(nc.sbuf_tensor(f"p1_wl{k}", [128, D], BF16)) for k in range(2)]
        evb = [ph(nc.sbuf_tensor(f"p1_ev{k}", [128, 1024], BF16)) for k in range(3)]
        evi = [0]
        for nm, src_h in (("q", wq_h), ("k", wk_h), ("v", wv_h), ("o", wo_h)):
            dsth, dstl = wTp[nm]
            for rt in range(ND):
                wb, wh, wl = wsb[rt % 2], whb[rt % 2], wlb[rt % 2]
                S.step("ld", lambda s, src_h=src_h, rt=rt, wb=wb: s.dma_start(out=wb[:], in_=src_h[128 * rt:128 * (rt + 1), :]), r=[src_h], w=[wb], dma="ld")
                S.step("vector", lambda v, wb=wb, wh=wh: v.tensor_copy(wh[:], wb[:]), r=[wb], w=[wh])
                S.step("vector", lambda v, wb=wb, wh=wh, wl=wl: v.tensor_tensor(out=wl[:], in0=wb[:], in1=wh[:], op=ALU.subtract), r=[wb, wh], w=[wl])
                for mat, dst in ((wh, dsth), (wl, dstl)):
                    ev = evb[evi[0] % 3]
                    evi[0] += 1
                    S.step("tensor", lambda t, mat=mat: [
                        t.transpose(psB16[:, 128 * j:128 * (j + 1)], mat[:, 128 * j:128 * (j + 1)], ebf[:])
                        for j in range(ND)], r=[mat, ebf], w=[psB16])
                    S.step("vector", lambda v, ev=ev: v.tensor_copy(ev[:], psB16[:]), r=[psB16], w=[ev])
                    S.step("st", lambda s, dst=dst, rt=rt, ev=ev: [
                        s.dma_start(out=dst[128 * j:128 * (j + 1), 128 * rt:128 * (rt + 1)], in_=ev[:, 128 * j:128 * (j + 1)])
                        for j in range(ND)], r=[ev], w=[dst], dma="st", n=ND)

        wsb = [ph(nc.sbuf_tensor(f"p1b_w{k}", [128, DFF], F32)) for k in range(2)]
        red = [ph(nc.sbuf_tensor(f"p1b_red{k}", [128, 1], F32)) for k in range(2)]
        acc = ph(nc.sbuf_tensor("p1b_acc", [128, 1], F32))
        tot = ph(nc.sbuf_tensor("p1b_tot", [1, 1], F32))
        ma = [ph(nc.sbuf_tensor(f"p1b_ma{k}", [128, DFF], F32)) for k in range(2)]
        mbq = [ph(nc.sbuf_tensor(f"p1b_mb{k}", [128, DFF], F32)) for k in range(2)]
        mi = [ph(nc.sbuf_tensor(f"p1b_mi{k}", [128, DFF], BF16)) for k in range(2)]
        evb = [ph(nc.sbuf_tensor(f"p1b_ev{k}", [128, 512], BF16)) for k in range(2)]

        for (w_hh, nrow, ncol, dstT, mbc, swbc) in (
                (w1_h, NF, D, w1qT_h, m1bc, sw1bc),
                (w2_h, ND, DFF, w2qT_h, m2bc, sw2bc)):
            for i in range(nrow):
                wb, rd = wsb[i % 2], red[i % 2]
                S.step("ld", lambda s, w_hh=w_hh, i=i, ncol=ncol, wb=wb: s.dma_start(out=wb[:, 0:ncol], in_=w_hh[128 * i:128 * (i + 1), :]), r=[w_hh], w=[wb], dma="ld")
                S.step("vector", lambda v, ncol=ncol, wb=wb, rd=rd: v.tensor_reduce(out=rd[:], in_=wb[:, 0:ncol], axis=AX.X, op=ALU.add, apply_absolute_value=True), r=[wb], w=[rd])
                if i == 0:
                    S.step("vector", lambda v, rd=rd: v.tensor_copy(acc[:], rd[:]), r=[rd], w=[acc])
                else:
                    S.step("vector", lambda v, rd=rd: v.tensor_tensor(out=acc[:], in0=acc[:], in1=rd[:], op=ALU.add), r=[acc, rd], w=[acc])
            S.step("tensor", lambda t: t.matmul(psP[3][0:1, 0:1], ones_c[:], acc[:], start=True, stop=True), r=[ones_c, acc], w=[psP[3]])
            S.step("vector", lambda v, nrow=nrow, ncol=ncol: v.tensor_scalar(
                out=tot[:], in0=psP[3][0:1, 0:1], scalar1=1.0 / (nrow * 128 * ncol), scalar2=1e-5, op0=ALU.mult, op1=ALU.max), r=[psP[3]], w=[tot])
            S.step("tensor", lambda t: t.matmul(psP[3][0:128, 0:1], ones_r[:], tot[:], start=True, stop=True), r=[ones_r, tot], w=[psP[3]])
            S.step("vector", lambda v, mbc=mbc: v.tensor_copy(mbc[:], psP[3][0:128, 0:1]), r=[psP[3]], w=[mbc])
            S.step("vector", lambda v, mbc=mbc, swbc=swbc: v.reciprocal(swbc[:], mbc[:]), r=[mbc], w=[swbc])
            for i in range(nrow):
                wb, maq, mbb, miq = wsb[i % 2], ma[i % 2], mbq[i % 2], mi[i % 2]
                S.step("ld", lambda s, w_hh=w_hh, i=i, ncol=ncol, wb=wb: s.dma_start(out=wb[:, 0:ncol], in_=w_hh[128 * i:128 * (i + 1), :]), r=[w_hh], w=[wb], dma="ld")
                S.step("vector", lambda g, ncol=ncol, swbc=swbc, wb=wb, maq=maq: g.tensor_scalar(
                    out=maq[:, 0:ncol], in0=wb[:, 0:ncol], scalar1=swbc[:], scalar2=0.5, op0=ALU.mult, op1=ALU.is_ge), r=[wb, swbc], w=[maq])
                S.step("vector", lambda g, ncol=ncol, swbc=swbc, wb=wb, mbb=mbb: g.tensor_scalar(
                    out=mbb[:, 0:ncol], in0=wb[:, 0:ncol], scalar1=swbc[:], scalar2=-0.5, op0=ALU.mult, op1=ALU.is_le), r=[wb, swbc], w=[mbb])
                S.step("vector", lambda v, ncol=ncol, maq=maq, mbb=mbb, miq=miq: v.tensor_tensor(
                    out=miq[:, 0:ncol], in0=maq[:, 0:ncol], in1=mbb[:, 0:ncol], op=ALU.subtract), r=[maq, mbb], w=[miq])
                for q in range(ncol // 512):
                    ev = evb[q % 2]
                    def ftr(t, q=q, miq=miq):
                        return [t.transpose(psB16[:, 128 * j:128 * (j + 1)],
                                            miq[:, 512 * q + 128 * j:512 * q + 128 * (j + 1)],
                                            ebf[:]) for j in range(4)]
                    S.step("tensor", ftr, r=[miq, ebf], w=[psB16])
                    S.step("vector", lambda vv, ev=ev: vv.tensor_copy(ev[:], psB16[:, 0:512]), r=[psB16], w=[ev])
                    def fst(s, dstT=dstT, q=q, i=i, ev=ev):
                        return [s.dma_start(
                            out=dstT[512 * q + 128 * j:512 * q + 128 * (j + 1), 128 * i:128 * (i + 1)],
                            in_=ev[:, 128 * j:128 * (j + 1)]) for j in range(4)]
                    S.step("st", fst, r=[ev], w=[dstT], dma="st", n=4)

    stkA = ExitStack()
    xnTh = [stkA.enter_context(nc.sbuf_tensor(f"xnTh{i}", [128, T], BF16)) for i in range(ND)]
    xnTl = [stkA.enter_context(nc.sbuf_tensor(f"xnTl{i}", [128, T], BF16)) for i in range(ND)]
    with Phase() as ph:
        NB = 2
        xsb = [ph(nc.sbuf_tensor(f"p0_x{k}", [128, D], F32)) for k in range(NB)]
        sqsb = [ph(nc.sbuf_tensor(f"p0_sq{k}", [128, D], F32)) for k in range(NB)]
        xn = [ph(nc.sbuf_tensor(f"p0_xn{k}", [128, D], F32)) for k in range(NB)]
        usb = [ph(nc.sbuf_tensor(f"p0_u{k}", [128, D], F32)) for k in range(NB)]
        cint = [ph(nc.sbuf_tensor(f"p0_ci{k}", [128, D], I32)) for k in range(NB)]
        cm1 = [ph(nc.sbuf_tensor(f"p0_cm1{k}", [128, D], F32)) for k in range(NB)]
        dsb = [ph(nc.sbuf_tensor(f"p0_d{k}", [128, D], F32)) for k in range(NB)]
        ainb = [ph(nc.sbuf_tensor(f"p0_ai{k}", [128, D], BF16)) for k in range(NB)]
        xnhb = [ph(nc.sbuf_tensor(f"p0_xnh{k}", [128, D], BF16)) for k in range(NB)]
        xnlb = [ph(nc.sbuf_tensor(f"p0_xnl{k}", [128, D], BF16)) for k in range(NB)]
        st8 = [ph(nc.sbuf_tensor(f"p0_st8{k}", [128, 8], F32)) for k in range(NB)]
        s1b = [ph(nc.sbuf_tensor(f"p0_s1{k}", [128, 1], F32)) for k in range(NB)]
        rs1b = [ph(nc.sbuf_tensor(f"p0_rs1{k}", [128, 1], F32)) for k in range(NB)]
        rcol = ph(nc.sbuf_tensor("p0_rcol", [128, 16], F32))
        rowp = ph(nc.sbuf_tensor("p0_rowp", [16, 128], F32))

        for tt in range(NT):
            k = tt % NB
            x_, sq_, xn_, u_, ci_, cm_, d_, ai_ = xsb[k], sqsb[k], xn[k], usb[k], cint[k], cm1[k], dsb[k], ainb[k]
            stat = st8[k]
            ssum, ssq, mean, m2v, var, stdv, rstd, amax = [stat[:, i:i + 1] for i in range(8)]
            s1, rs1 = s1b[k], rs1b[k]
            V = "vector"
            S.step("ld", lambda s, tt=tt, x_=x_: s.dma_start(out=x_[:], in_=x_h[128 * tt:128 * (tt + 1), :]), r=[x_h], w=[x_], dma="ld")
            S.step(V, lambda v, x_=x_, stat=stat: v.tensor_reduce(out=stat[:, 0:1], in_=x_[:], axis=AX.X, op=ALU.add), r=[x_], w=[stat])
            S.step("vector", lambda v, x_=x_, sq_=sq_: v.tensor_tensor(out=sq_[:], in0=x_[:], in1=x_[:], op=ALU.mult), r=[x_], w=[sq_])
            S.step(V, lambda v, sq_=sq_, stat=stat: v.tensor_reduce(out=stat[:, 1:2], in_=sq_[:], axis=AX.X, op=ALU.add), r=[sq_, stat], w=[stat])
            S.step(V, lambda v, stat=stat: v.tensor_scalar(out=stat[:, 2:3], in0=stat[:, 0:1], scalar1=1.0 / D, scalar2=None, op0=ALU.mult), r=[stat], w=[stat])
            S.step(V, lambda v, stat=stat: v.tensor_tensor(out=stat[:, 3:4], in0=stat[:, 2:3], in1=stat[:, 2:3], op=ALU.mult), r=[stat], w=[stat])
            S.step(V, lambda v, stat=stat: v.scalar_tensor_tensor(out=stat[:, 4:5], in0=stat[:, 1:2], scalar=1.0 / D, in1=stat[:, 3:4], op0=ALU.mult, op1=ALU.subtract), r=[stat], w=[stat])
            S.step(V, lambda v, stat=stat: v.tensor_scalar(out=stat[:, 4:5], in0=stat[:, 4:5], scalar1=1e-5, scalar2=None, op0=ALU.add), r=[stat], w=[stat])
            S.step("scalar", lambda s, stat=stat: s.activation(out=stat[:, 5:6], in_=stat[:, 4:5], func=AF.Sqrt), r=[stat], w=[stat])
            S.step(V, lambda v, stat=stat: v.reciprocal(stat[:, 6:7], stat[:, 5:6]), r=[stat], w=[stat])
            S.step(V, lambda v, x_=x_, xn_=xn_, stat=stat: v.tensor_scalar(out=xn_[:], in0=x_[:], scalar1=stat[:, 2:3], scalar2=stat[:, 6:7], op0=ALU.subtract, op1=ALU.mult), r=[x_, stat], w=[xn_])
            S.step(V, lambda v, xn_=xn_, stat=stat: v.tensor_reduce(out=stat[:, 7:8], in_=xn_[:], axis=AX.X, op=ALU.max, apply_absolute_value=True), r=[xn_, stat], w=[stat])
            S.step(V, lambda v, stat=stat, s1=s1: v.tensor_scalar(out=s1[:], in0=stat[:, 7:8], scalar1=1e-5, scalar2=None, op0=ALU.max), r=[stat], w=[s1])
            S.step(V, lambda v, s1=s1: v.reciprocal(s1[:], s1[:]), r=[s1], w=[s1])
            S.step(V, lambda v, s1=s1: v.tensor_scalar(out=s1[:], in0=s1[:], scalar1=127.0, scalar2=None, op0=ALU.mult), r=[s1], w=[s1])
            S.step(V, lambda v, s1=s1, rs1=rs1: v.reciprocal(rs1[:], s1[:]), r=[s1], w=[rs1])
            S.step(V, lambda v, rs1=rs1, tt=tt: v.tensor_copy(rcol[:, tt:tt + 1], rs1[:]), r=[rs1, rcol], w=[rcol])
            S.step(V, lambda v, xn_=xn_, u_=u_, s1=s1: v.tensor_scalar(out=u_[:], in0=xn_[:], scalar1=s1[:], scalar2=None, op0=ALU.mult), r=[xn_, s1], w=[u_])
            S.step(V, lambda v, u_=u_, ci_=ci_: v.tensor_copy(ci_[:], u_[:]), r=[u_], w=[ci_])
            S.step(V, lambda v, ci_=ci_, cm_=cm_: v.tensor_scalar(out=cm_[:], in0=ci_[:], scalar1=-1.0, scalar2=None, op0=ALU.add), r=[ci_], w=[cm_])
            S.step(V, lambda v, u_=u_, cm_=cm_, d_=d_: v.scalar_tensor_tensor(out=d_[:], in0=u_[:], scalar=-1.0, in1=cm_[:], op0=ALU.add, op1=ALU.subtract), r=[u_, cm_], w=[d_])
            S.step(V, lambda v, d_=d_, cm_=cm_, u_=u_: v.scalar_tensor_tensor(out=u_[:], in0=d_[:], scalar=0.5, in1=cm_[:], op0=ALU.is_ge, op1=ALU.add), r=[d_, cm_], w=[u_])
            S.step(V, lambda v, d_=d_, u_=u_, ai_=ai_: v.scalar_tensor_tensor(out=ai_[:], in0=d_[:], scalar=-0.5, in1=u_[:], op0=ALU.is_gt, op1=ALU.add), r=[d_, u_], w=[ai_])
            xnh_, xnl_ = xnhb[k], xnlb[k]
            S.step("vector", lambda v, xn_=xn_, xnh_=xnh_: v.tensor_copy(xnh_[:], xn_[:]), r=[xn_], w=[xnh_])
            S.step("vector", lambda v, xn_=xn_, xnh_=xnh_, xnl_=xnl_: v.tensor_tensor(out=xnl_[:], in0=xn_[:], in1=xnh_[:], op=ALU.subtract), r=[xn_, xnh_], w=[xnl_])
            for mat, dstl in ((xnh_, xnTh), (xnl_, xnTl), (ai_, aT)):
                S.step("tensor", lambda t, mat=mat: [
                    t.transpose(psB16[:, 128 * j:128 * (j + 1)], mat[:, 128 * j:128 * (j + 1)], ebf[:])
                    for j in range(ND)], r=[mat, ebf], w=[psB16])
                S.step("vector", lambda v, dstl=dstl, tt=tt: [
                    v.tensor_copy(dstl[j][:, 128 * tt:128 * (tt + 1)], psB16[:, 128 * j:128 * (j + 1)])
                    for j in range(ND)], r=[psB16] + dstl, w=dstl)
        S.step("tensor", lambda t: t.transpose(psP[2][0:16, 0:128], rcol[:], eye[:]), r=[rcol, eye], w=[psP[2]])
        S.step("vector", lambda v: v.tensor_copy(rowp[:], psP[2][0:16, 0:128]), r=[psP[2]], w=[rowp])
        S.step("st", lambda s: [s.dma_start(out=R1ROW[0:1, 128 * j:128 * (j + 1)], in_=rowp[j:j + 1, :]) for j in range(16)], r=[rowp], w=[R1ROW], dma="st", n=16)

    with Phase() as ph:
        shb = [ph(nc.sbuf_tensor(f"p2_sh{k}", [128, D], BF16)) for k in range(2)]
        slb = [ph(nc.sbuf_tensor(f"p2_sl{k}", [128, D], BF16)) for k in range(2)]
        evh = [ph(nc.sbuf_tensor(f"p2_evh{k}", [128, 512], BF16)) for k in range(3)]
        evl = [ph(nc.sbuf_tensor(f"p2_evl{k}", [128, 512], BF16)) for k in range(3)]
        n = 0
        for nm, dstp in (("q", QTp_h), ("k", KTp_h), ("v", VTp_h)):
            srch, srcl = wTp[nm]
            for of in range(ND):
                sh, sl = shb[n % 2], slb[n % 2]
                n += 1
                srchr = srch.rearrange("(c p) t -> p c t", p=128)
                srclr = srcl.rearrange("(c p) t -> p c t", p=128)
                S.step("ld", lambda s, srchr=srchr, srclr=srclr, of=of, sh=sh, sl=sl: [
                    s.dma_start(out=sh[:].rearrange("p (c t) -> p c t", c=ND),
                                in_=srchr[:, :, 128 * of:128 * (of + 1)]),
                    s.dma_start(out=sl[:].rearrange("p (c t) -> p c t", c=ND),
                                in_=srclr[:, :, 128 * of:128 * (of + 1)])],
                    r=[srch, srcl], w=[sh, sl], dma="ld", n=2)
                for tg in range(NG):
                    ps = psP[tg % 2]
                    def fmm(t, tg=tg, sh=sh, sl=sl, ps=ps):
                        mm = []
                        for i in range(ND):
                            ci = slice(128 * i, 128 * (i + 1))
                            mm.append(t.matmul(ps[:], sh[:, ci], xnTh[i][:, 512 * tg:512 * (tg + 1)], start=(i == 0), stop=False))
                            mm.append(t.matmul(ps[:], sh[:, ci], xnTl[i][:, 512 * tg:512 * (tg + 1)], start=False, stop=False))
                            mm.append(t.matmul(ps[:], sl[:, ci], xnTh[i][:, 512 * tg:512 * (tg + 1)], start=False, stop=(i == ND - 1)))
                        return mm
                    S.step("tensor", fmm, r=[sh, sl] + xnTh + xnTl, w=[ps])
                    eh, el = evh[tg % 3], evl[tg % 3]
                    S.step("vector", lambda v, ps=ps, eh=eh: v.tensor_copy(eh[:], ps[:]), r=[ps], w=[eh])
                    S.step("vector", lambda v, ps=ps, eh=eh, el=el: v.tensor_tensor(out=el[:], in0=ps[:], in1=eh[:], op=ALU.subtract), r=[ps, eh], w=[el])
                    S.step("st", lambda s, dstp=dstp, of=of, tg=tg, eh=eh, el=el: [
                        s.dma_start(out=dstp[0][128 * of:128 * (of + 1), 512 * tg:512 * (tg + 1)], in_=eh[:]),
                        s.dma_start(out=dstp[1][128 * of:128 * (of + 1), 512 * tg:512 * (tg + 1)], in_=el[:])],
                        r=[eh, el], w=[dstp[0], dstp[1]], dma="st", n=2)
    stkA.close()

    with Phase() as ph:
        NBH = 2
        K2A = [ph(nc.sbuf_tensor(f"p3_k2a{k}", [128, T], BF16)) for k in range(NBH)]
        K2B = [ph(nc.sbuf_tensor(f"p3_k2b{k}", [128, T], BF16)) for k in range(NBH)]
        Q2A = [ph(nc.sbuf_tensor(f"p3_q2a{k}", [128, T], BF16)) for k in range(NBH)]
        Q2B = [ph(nc.sbuf_tensor(f"p3_q2b{k}", [128, T], BF16)) for k in range(NBH)]
        QLA = [ph(nc.sbuf_tensor(f"p3_qla{k}", [128, T], BF16)) for k in range(NBH)]
        QLB = [ph(nc.sbuf_tensor(f"p3_qlb{k}", [128, T], BF16)) for k in range(NBH)]
        vph = ph(nc.sbuf_tensor("p3_vh", [128, T], BF16))
        vpl = ph(nc.sbuf_tensor("p3_vl", [128, T], BF16))
        sq2 = ph(nc.sbuf_tensor("p3_sq", [128, T], BF16))
        stb = [[ph(nc.sbuf_tensor(f"p3_st{h2}{k}", [1, 16], F32)) for h2 in range(2)] for k in range(NBH)]
        negMb = [[ph(nc.sbuf_tensor(f"p3_nM{h2}{k}", [128, 1], F32)) for h2 in range(2)] for k in range(NBH)]
        vchb = [[[ph(nc.sbuf_tensor(f"p3_vch{h2}{p}_{i}", [128, 65], BF16)) for i in range(NT)]
                 for p in range(2)] for h2 in range(2)]
        ptf = [ph(nc.sbuf_tensor(f"p3_ptf{k}", [128, 512], F32)) for k in range(4)]
        pth = [ph(nc.sbuf_tensor(f"p3_pth{k}", [128, 512], BF16)) for k in range(4)]
        ptl = [ph(nc.sbuf_tensor(f"p3_ptl{k}", [128, 512], BF16)) for k in range(4)]
        rsb = [ph(nc.sbuf_tensor(f"p3_rs{k}", [1, 512], F32)) for k in range(2)]
        rrb = [ph(nc.sbuf_tensor(f"p3_rr{k}", [64, 512], F32)) for k in range(2)]
        otm = [ph(nc.sbuf_tensor(f"p3_ot{k}", [64, 512], F32)) for k in range(2)]
        othi = [ph(nc.sbuf_tensor(f"p3_oth{k}", [64, 512], BF16)) for k in range(2)]
        otlo = [ph(nc.sbuf_tensor(f"p3_otl{k}", [64, 512], BF16)) for k in range(2)]

        for k in range(NBH):
            S.step("gpsimd", lambda g, k=k: g.memset(QLA[k][64:128, :], 0.0), w=[QLA[k]])
            S.step("gpsimd", lambda g, k=k: g.memset(QLB[k][0:64, :], 0.0), w=[QLB[k]])

        for hp in range(ND):
            k = hp % NBH
            k2a, k2b, q2a, q2b, qla, qlb = K2A[k], K2B[k], Q2A[k], Q2B[k], QLA[k], QLB[k]
            rA = slice(128 * hp, 128 * hp + 64)
            rB = slice(128 * hp + 64, 128 * hp + 128)
            S.step("ld", lambda s, k2a=k2a, k2b=k2b, q2a=q2a, q2b=q2b, rA=rA, rB=rB: [
                s.dma_start(out=k2a[0:64, :], in_=KTp_h[0][rA, :]),
                s.dma_start(out=k2a[64:128, :], in_=KTp_h[1][rA, :]),
                s.dma_start(out=k2b[0:64, :], in_=KTp_h[1][rB, :]),
                s.dma_start(out=k2b[64:128, :], in_=KTp_h[0][rB, :]),
                s.dma_start(out=q2a[0:64, :], in_=QTp_h[0][rA, :]),
                s.dma_start(out=q2a[64:128, :], in_=QTp_h[0][rA, :]),
                s.dma_start(out=q2b[0:64, :], in_=QTp_h[0][rB, :]),
                s.dma_start(out=q2b[64:128, :], in_=QTp_h[0][rB, :])],
                r=[KTp_h[0], KTp_h[1], QTp_h[0]], w=[k2a, k2b, q2a, q2b], dma="ld", n=8)
            S.step("ld", lambda s, qla=qla, qlb=qlb, rA=rA, rB=rB, hp=hp: [
                s.dma_start(out=qla[0:64, :], in_=QTp_h[1][rA, :]),
                s.dma_start(out=qlb[64:128, :], in_=QTp_h[1][rB, :]),
                s.dma_start(out=vph[:], in_=VTp_h[0][128 * hp:128 * (hp + 1), :]),
                s.dma_start(out=vpl[:], in_=VTp_h[1][128 * hp:128 * (hp + 1), :])],
                r=[QTp_h[1], VTp_h[0], VTp_h[1], qla, qlb, vph, vpl], w=[qla, qlb, vph, vpl], dma="ld", n=4)
            for h2 in range(2) if 'norm' not in p3skip else []:
                ro = slice(0, 64) if h2 == 0 else slice(64, 128)
                qsrc = q2a if h2 == 0 else q2b
                ksrc = k2a if h2 == 0 else k2b
                st4 = stb[k][h2]
                negM = negMb[k][h2]
                for si, srcp in ((0, qsrc), (1, ksrc)):
                    S.step("vector", lambda v, srcp=srcp, ro=ro: v.tensor_tensor(out=sq2[ro, :], in0=srcp[ro, :], in1=srcp[ro, :], op=ALU.mult), r=[srcp, sq2], w=[sq2])
                    for g in range(NG):
                        S.step("tensor", lambda t, g=g, ro=ro: t.matmul(
                            psP[4][0:1, 0:512], ones_bf[ro, :], sq2[ro, 512 * g:512 * (g + 1)], start=True, stop=True),
                            r=[ones_bf, sq2], w=[psP[4]])
                        S.step("vector", lambda v, g=g, si=si, st4=st4: v.tensor_reduce(
                            out=st4[:, 4 * si + g:4 * si + g + 1], in_=psP[4][0:1, 0:512], axis=AX.X, op=ALU.max),
                            r=[psP[4], st4], w=[st4])
                S.step("vector", lambda v, st4=st4: v.tensor_reduce(out=st4[:, 8:9], in_=st4[:, 0:4], axis=AX.X, op=ALU.max), r=[st4], w=[st4])
                S.step("vector", lambda v, st4=st4: v.tensor_reduce(out=st4[:, 9:10], in_=st4[:, 4:8], axis=AX.X, op=ALU.max), r=[st4], w=[st4])
                S.step("vector", lambda v, st4=st4: v.tensor_tensor(out=st4[:, 10:11], in0=st4[:, 8:9], in1=st4[:, 9:10], op=ALU.add), r=[st4], w=[st4])
                S.step("vector", lambda v, st4=st4: v.tensor_scalar(out=st4[:, 11:12], in0=st4[:, 10:11], scalar1=-0.06375, scalar2=None, op0=ALU.mult), r=[st4], w=[st4])
                S.step("tensor", lambda t, st4=st4: t.matmul(psP[4][0:128, 0:1], ones_r[:], st4[:, 11:12], start=True, stop=True), r=[ones_r, st4], w=[psP[4]])
                S.step("vector", lambda v, negM=negM: v.tensor_copy(negM[:], psP[4][0:128, 0:1]), r=[psP[4]], w=[negM])
            for kc in range(NT) if 'vch' not in p3skip else []:
                S.step("tensor", lambda t, kc=kc: [
                    t.transpose(psB16[:, 0:128], vph[:, 128 * kc:128 * (kc + 1)], ebf[:]),
                    t.transpose(psB16[:, 128:256], vpl[:, 128 * kc:128 * (kc + 1)], ebf[:])],
                    r=[vph, vpl, ebf], w=[psB16])
                def fev(v, kc=kc):
                    ops = []
                    for idx, (h2, p) in enumerate(((0, 0), (1, 0), (0, 1), (1, 1))):
                        vt = vchb[h2][p][kc]
                        ops.append(v.tensor_copy(vt[:, 0:64], psB16[:, 128 * p + 64 * h2:128 * p + 64 * (h2 + 1)]))
                        ops.append(v.tensor_copy(vt[:, 64:65], ones_bf[:] if p == 0 else zeros_bf[:]))
                    return ops
                vts = [vchb[h2][p][kc] for h2 in range(2) for p in range(2)]
                S.step("vector", fev, r=[psB16, ones_bf, zeros_bf] + vts, w=vts)
            for qt in range(NG) if 'attn' not in p3skip else []:
                nkc = 4 * (qt + 1)

                def logits(kc, qt=qt, k=k):
                    for h2 in range(2):
                        psX = psP[2 * (kc % 2) + h2]
                        ksrc = (K2A, K2B)[h2][k]
                        qsrc = (Q2A, Q2B)[h2][k]
                        qlo = (QLA, QLB)[h2][k]
                        def fmm(t, kc=kc, psX=psX, ksrc=ksrc, qsrc=qsrc, qlo=qlo):
                            return [
                                t.matmul(psX[:], ksrc[:, 128 * kc:128 * (kc + 1)], qsrc[:, 512 * qt:512 * (qt + 1)], start=True, stop=False),
                                t.matmul(psX[:], ksrc[:, 128 * kc:128 * (kc + 1)], qlo[:, 512 * qt:512 * (qt + 1)], start=False, stop=True)]
                        S.step("tensor", fmm, r=[ksrc, qsrc, qlo], w=[psX])

                def mask(kc, qt=qt):
                    if kc >= 4 * qt:
                        rel = kc - 4 * qt
                        wmask = 128 * (rel + 1)
                        for h2 in range(2):
                            psX = psP[2 * (kc % 2) + h2]
                            S.step("vector", lambda v, rel=rel, psX=psX, wmask=wmask: v.tensor_tensor(
                                out=psX[:, 0:wmask], in0=psX[:, 0:wmask], in1=masks[rel][:, 0:wmask], op=ALU.add), r=[psX, masks[rel]], w=[psX])

                def epi(kc, qt=qt, nkc=nkc, k=k):
                    for h2 in range(2):
                        psX = psP[2 * (kc % 2) + h2]
                        psO = psP[5 + h2]
                        pf = ptf[2 * (kc % 2) + h2]
                        phh = pth[2 * (kc % 2) + h2]
                        pll = ptl[2 * (kc % 2) + h2]
                        negM = negMb[k][h2]
                        S.step("scalar", lambda s, psX=psX, pf=pf, negM=negM: s.activation(out=pf[:], in_=psX[:], func=AF.Exp, scale=0.125, bias=negM[:]), r=[psX, negM], w=[pf])
                        if h2 == 0:
                            S.step("scalar", lambda s, pf=pf, phh=phh: s.copy(phh[:], pf[:]), r=[pf, phh], w=[phh])
                            S.step("vector", lambda v, pf=pf, phh=phh, pll=pll: v.tensor_tensor(out=pll[:], in0=pf[:], in1=phh[:], op=ALU.subtract), r=[pf, phh, pll], w=[pll])
                        else:
                            S.step("vector", lambda v, pf=pf, phh=phh: v.tensor_copy(phh[:], pf[:]), r=[pf, phh], w=[phh])
                            S.step("vector", lambda g, pf=pf, phh=phh, pll=pll: g.tensor_tensor(out=pll[:], in0=pf[:], in1=phh[:], op=ALU.subtract), r=[pf, phh, pll], w=[pll])
                        vh_ = vchb[h2][0][kc]
                        vl_ = vchb[h2][1][kc]
                        def fav(t, kc=kc, nkc=nkc, vh_=vh_, vl_=vl_, phh=phh, pll=pll, psO=psO):
                            return [
                                t.matmul(psO[0:65, :], vh_[:], phh[:], start=(kc == 0), stop=False),
                                t.matmul(psO[0:65, :], vh_[:], pll[:], start=False, stop=False),
                                t.matmul(psO[0:65, :], vl_[:], phh[:], start=False, stop=(kc == nkc - 1))]
                        S.step("tensor", fav, r=[vh_, vl_, phh, pll], w=[psO])

                logits(0)
                mask(0)
                for kc in range(1, nkc):
                    logits(kc)
                    mask(kc)
                    epi(kc - 1)
                epi(nkc - 1)
                for h2 in range(2):
                    psO = psP[5 + h2]
                    rs = rsb[h2]
                    rr = rrb[h2]
                    ot = otm[h2]
                    S.step("vector", lambda v, rs=rs, psO=psO: v.reciprocal(rs[:], psO[64:65, :]), r=[psO, rs], w=[rs])
                    S.step("tensor", lambda t, rs=rs: t.matmul(psP[4][0:64, 0:512], ones_r[:, 0:64], rs[:], start=True, stop=True), r=[ones_r, rs], w=[psP[4]])
                    S.step("vector", lambda v, rr=rr: v.tensor_copy(rr[:], psP[4][0:64, 0:512]), r=[psP[4], rr], w=[rr])
                    S.step("vector", lambda v, rr=rr, ot=ot, psO=psO: v.tensor_tensor(out=ot[:], in0=psO[0:64, :], in1=rr[:], op=ALU.mult), r=[psO, rr, ot], w=[ot])
                    ohi, olo = othi[h2], otlo[h2]
                    S.step("vector", lambda v, ot=ot, ohi=ohi: v.tensor_copy(ohi[:], ot[:]), r=[ot, ohi], w=[ohi])
                    S.step("vector", lambda v, ot=ot, ohi=ohi, olo=olo: v.tensor_tensor(out=olo[:], in0=ot[:], in1=ohi[:], op=ALU.subtract), r=[ot, ohi, olo], w=[olo])
                    po = 128 * hp + 64 * h2
                    S.step("st", lambda s, ohi=ohi, olo=olo, po=po, qt=qt: [
                        s.dma_start(out=OCTH_h[po:po + 64, 512 * qt:512 * (qt + 1)], in_=ohi[:]),
                        s.dma_start(out=OCTL_h[po:po + 64, 512 * qt:512 * (qt + 1)], in_=olo[:])],
                        r=[ohi, olo], w=[OCTH_h, OCTL_h], dma="st", n=2)

    with Phase() as ph:
        shb = [[ph(nc.sbuf_tensor(f"p4_sh{k}_{i}", [128, 128], BF16)) for i in range(ND)] for k in range(2)]
        slb = [[ph(nc.sbuf_tensor(f"p4_sl{k}_{i}", [128, 128], BF16)) for i in range(ND)] for k in range(2)]
        ocf = [ph(nc.sbuf_tensor(f"p4_oh_{i}", [128, T], BF16)) for i in range(ND)]
        olf = [ph(nc.sbuf_tensor(f"p4_ol_{i}", [128, T], BF16)) for i in range(ND)]
        evb = [ph(nc.sbuf_tensor(f"p4_ev{k}", [128, 512], F32)) for k in range(3)]
        woh, wol = wTp["o"]
        S.step("ld", lambda s: [s.dma_start(out=ocf[i][:], in_=OCTH_h[128 * i:128 * (i + 1), :]) for i in range(ND)]
               + [s.dma_start(out=olf[i][:], in_=OCTL_h[128 * i:128 * (i + 1), :]) for i in range(ND)],
               r=[OCTH_h, OCTL_h], w=ocf + olf, dma="ld", n=2 * ND)
        for of in range(ND):
            sh, sl = shb[of % 2], slb[of % 2]
            S.step("ld", lambda s, of=of, sh=sh, sl=sl: [
                s.dma_start(out=sh[i][:], in_=woh[128 * i:128 * (i + 1), 128 * of:128 * (of + 1)])
                for i in range(ND)] + [
                s.dma_start(out=sl[i][:], in_=wol[128 * i:128 * (i + 1), 128 * of:128 * (of + 1)])
                for i in range(ND)], r=[woh, wol], w=sh + sl, dma="ld", n=2 * ND)
            for tg in range(NG):
                ps = psP[tg % 2]
                ev = evb[tg % 3]
                def fmm(t, tg=tg, sh=sh, sl=sl, ps=ps):
                    mm = []
                    for i in range(ND):
                        sli = slice(512 * tg, 512 * (tg + 1))
                        mm.append(t.matmul(ps[:], sh[i][:], ocf[i][:, sli], start=(i == 0), stop=False))
                        mm.append(t.matmul(ps[:], sh[i][:], olf[i][:, sli], start=False, stop=False))
                        mm.append(t.matmul(ps[:], sl[i][:], ocf[i][:, sli], start=False, stop=(i == ND - 1)))
                    return mm
                S.step("tensor", fmm, r=sh + sl + ocf + olf, w=[ps])
                S.step("vector", lambda v, ps=ps, ev=ev: v.tensor_copy(ev[:], ps[:]), r=[ps], w=[ev])
                S.step("st", lambda s, of=of, tg=tg, ev=ev: s.dma_start(
                    out=ATT_h[128 * of:128 * (of + 1), 512 * tg:512 * (tg + 1)], in_=ev[:]), r=[ev], w=[ATT_h], dma="st")

        statb = [[ph(nc.sbuf_tensor(f"p5_st{k}_{i}", [128, 128], BF16)) for i in range(ND)] for k in range(2)]
        hb = [ph(nc.sbuf_tensor(f"p5_h{k}", [128, T], F32)) for k in range(2)]
        yb = [ph(nc.sbuf_tensor(f"p5_y{k}", [128, T], F32)) for k in range(2)]
        gb = [ph(nc.sbuf_tensor(f"p5_g{k}", [128, T], F32)) for k in range(2)]
        gabb = [ph(nc.sbuf_tensor("p5_gab0", [128, T], F32))] * 2
        M1 = ph(nc.sbuf_tensor("p5_M1", [128, T], F32))
        cwb = [ph(nc.sbuf_tensor(f"p5_cw{k}", [128, 3], F32)) for k in range(2)]
        cbb = [ph(nc.sbuf_tensor(f"p5_cb{k}", [128, 1], F32)) for k in range(2)]
        R1REP = ph(nc.sbuf_tensor("p5_R1REP", [128, T], F32))
        r1row_sb = ph(nc.sbuf_tensor("p5_r1row", [1, T], F32))
        S.step("ld", lambda s: s.dma_start(out=r1row_sb[:], in_=R1ROW[:]), r=[R1ROW], w=[r1row_sb], dma="ld")
        for g in range(NG):
            S.step("tensor", lambda t, g=g: t.matmul(psP[4][:], ones_r[:], r1row_sb[0:1, 512 * g:512 * (g + 1)], start=True, stop=True), r=[ones_r, r1row_sb], w=[psP[4]])
            S.step("vector", lambda v, g=g: v.tensor_copy(R1REP[:, 512 * g:512 * (g + 1)], psP[4][:]), r=[psP[4], R1REP], w=[R1REP])
        for i in range(NF):
            k = i % 2
            stat, hs, ys, gs, gab = statb[k], hb[k], yb[k], gb[k], gabb[k]
            cwt, cbt = cwb[k], cbb[k]
            S.step("ld", lambda s, i=i, stat=stat, cwt=cwt, cbt=cbt: [
                s.dma_start(out=stat[j][:], in_=w1qT_h[128 * j:128 * (j + 1), 128 * i:128 * (i + 1)])
                for j in range(ND)] + [
                s.dma_start(out=cwt[:], in_=cw_h[128 * i:128 * (i + 1), :]),
                s.dma_start(out=cbt[:], in_=cb_h[128 * i:128 * (i + 1)].rearrange("(p o) -> p o", o=1))],
                r=[w1qT_h, cw_h, cb_h], w=stat + [cwt, cbt], dma="ld", n=ND + 2)
            for tg in range(NG):
                ps = psP[tg % 2]
                def fmm(t, tg=tg, stat=stat, ps=ps):
                    return [t.matmul(ps[:], stat[j][:], aT[j][:, 512 * tg:512 * (tg + 1)],
                                     start=(j == 0), stop=(j == ND - 1)) for j in range(ND)]
                S.step("tensor", fmm, r=stat + aT, w=[ps])
                S.step("vector", lambda v, tg=tg, ps=ps, hs=hs: v.scalar_tensor_tensor(
                    out=hs[:, 512 * tg:512 * (tg + 1)], in0=ps[:], scalar=m1bc[:], in1=R1REP[:, 512 * tg:512 * (tg + 1)],
                    op0=ALU.mult, op1=ALU.mult), r=[ps, m1bc, R1REP, hs], w=[hs])
            S.step("vector", lambda v, hs=hs, ys=ys, cwt=cwt, cbt=cbt: v.tensor_scalar(out=ys[:], in0=hs[:], scalar1=cwt[:, 1:2], scalar2=cbt[:], op0=ALU.mult, op1=ALU.add), r=[hs, cwt, cbt], w=[ys])
            S.step("vector", lambda v, hs=hs, ys=ys, cwt=cwt: v.scalar_tensor_tensor(out=ys[:, 1:T], in0=hs[:, 0:T - 1], scalar=cwt[:, 0:1], in1=ys[:, 1:T], op0=ALU.mult, op1=ALU.add), r=[hs, ys, cwt], w=[ys])
            S.step("vector", lambda v, hs=hs, ys=ys, cwt=cwt: v.scalar_tensor_tensor(out=ys[:, 0:T - 1], in0=hs[:, 1:T], scalar=cwt[:, 2:3], in1=ys[:, 0:T - 1], op0=ALU.mult, op1=ALU.add), r=[hs, ys, cwt], w=[ys])
            S.step("scalar", lambda s, ys=ys, gs=gs: s.activation(out=gs[:], in_=ys[:], func=AF.Silu), r=[ys], w=[gs])
            if i == 0:
                S.step("vector", lambda v, gs=gs: v.scalar_tensor_tensor(out=M1[:], in0=gs[:], scalar=-1.0, in1=gs[:], op0=ALU.mult, op1=ALU.max), r=[gs], w=[M1])
            else:
                S.step("vector", lambda v, gs=gs, gab=gab: v.scalar_tensor_tensor(out=gab[:], in0=gs[:], scalar=-1.0, in1=gs[:], op0=ALU.mult, op1=ALU.max), r=[gs], w=[gab])
                S.step("vector", lambda v, gab=gab: v.tensor_tensor(out=M1[:], in0=M1[:], in1=gab[:], op=ALU.max), r=[M1, gab], w=[M1])
            S.step("st", lambda s, i=i, gs=gs: s.dma_start(out=G_h[128 * i:128 * (i + 1), :], in_=gs[:]), r=[gs], w=[G_h], dma="st")

        mcol = ph(nc.sbuf_tensor("p5_mcol", [128, 32], F32))
        rowp = ph(nc.sbuf_tensor("p5_rowp", [32, 128], F32))
        for j in range(NT):
            S.step("tensor", lambda t, j=j: t.transpose(psP[5][:, 0:128], M1[:, 128 * j:128 * (j + 1)], eye[:]), r=[M1, eye], w=[psP[5]])
            S.step("vector", lambda v, j=j: v.tensor_reduce(out=mcol[:, j:j + 1], in_=psP[5][:, 0:128], axis=AX.X, op=ALU.max), r=[psP[5], mcol], w=[mcol])
        S.step("vector", lambda v: v.tensor_scalar(out=mcol[:, 0:16], in0=mcol[:, 0:16], scalar1=1e-5, scalar2=None, op0=ALU.max), r=[mcol], w=[mcol])
        S.step("vector", lambda v: v.reciprocal(mcol[:, 16:32], mcol[:, 0:16]), r=[mcol], w=[mcol])
        S.step("vector", lambda v: v.tensor_scalar(out=mcol[:, 16:32], in0=mcol[:, 16:32], scalar1=127.0, scalar2=None, op0=ALU.mult), r=[mcol], w=[mcol])
        S.step("vector", lambda v: v.reciprocal(mcol[:, 0:16], mcol[:, 16:32]), r=[mcol], w=[mcol])
        S.step("tensor", lambda t: t.transpose(psP[5][0:32, 0:128], mcol[:, 0:32], eye[:]), r=[mcol, eye], w=[psP[5]])
        S.step("vector", lambda v: v.tensor_copy(rowp[:], psP[5][0:32, 0:128]), r=[psP[5]], w=[rowp])
        S.step("st", lambda s: [s.dma_start(out=S2ROW[0:1, 128 * j:128 * (j + 1)], in_=rowp[16 + j:17 + j, :]) for j in range(16)], r=[rowp], w=[S2ROW], dma="st", n=16)
        S.step("st", lambda s: [s.dma_start(out=RS2ROW[0:1, 128 * j:128 * (j + 1)], in_=rowp[j:j + 1, :]) for j in range(16)], r=[rowp], w=[RS2ROW], dma="st", n=16)

    with Phase() as ph:
        w2sb = [ph(nc.sbuf_tensor(f"p5c_w2_{i}", [128, D], BF16)) for i in range(NF)]
        gqb = [[ph(nc.sbuf_tensor(f"p5c_gq{kk}_{i}", [128, 512], BF16)) for i in range(NF)] for kk in range(2)]
        glb = [ph(nc.sbuf_tensor(f"p5c_gl{k}", [128, 512], F32)) for k in range(2)]
        ubb = [ph(nc.sbuf_tensor(f"p5c_u{k}", [128, 512], F32)) for k in range(2)]
        cib = [ph(nc.sbuf_tensor(f"p5c_ci{k}", [128, 512], I32)) for k in range(2)]
        cm1b = [ph(nc.sbuf_tensor(f"p5c_cm1{k}", [128, 512], F32)) for k in range(2)]
        dbb = [ph(nc.sbuf_tensor(f"p5c_d{k}", [128, 512], F32)) for k in range(2)]
        evb = [ph(nc.sbuf_tensor(f"p5c_ev{k}", [128, 512], F32)) for k in range(3)]
        S2repb = [ph(nc.sbuf_tensor(f"p5c_S2{kk}", [128, 512], F32)) for kk in range(2)]
        RS2repb = [ph(nc.sbuf_tensor(f"p5c_RS2{kk}", [128, 512], F32)) for kk in range(2)]
        S.step("ld", lambda s: [s.dma_start(out=w2sb[i][:], in_=w2qT_h[128 * i:128 * (i + 1), :]) for i in range(NF)],
               r=[w2qT_h], w=w2sb, dma="ld", n=NF)
        srowb = [ph(nc.sbuf_tensor("p5c_srow0", [1, 1024], F32))] * 2
        for tb in range(NG):
            gq = gqb[tb % 2]
            S2rep, RS2rep = S2repb[tb % 2], RS2repb[tb % 2]
            srow_l = srowb[tb % 2]
            S.step("ld", lambda s, tb=tb: [
                s.dma_start(out=srow_l[0:1, 0:512], in_=S2ROW[0:1, 512 * tb:512 * (tb + 1)]),
                s.dma_start(out=srow_l[0:1, 512:1024], in_=RS2ROW[0:1, 512 * tb:512 * (tb + 1)])],
                r=[S2ROW, RS2ROW], w=[srow_l], dma="ld", n=2)
            S.step("tensor", lambda t: t.matmul(psP[4][:], ones_r[:], srow_l[0:1, 0:512], start=True, stop=True), r=[ones_r, srow_l], w=[psP[4]])
            S.step("vector", lambda v, S2rep=S2rep: v.tensor_copy(S2rep[:], psP[4][:]), r=[psP[4]], w=[S2rep])
            S.step("tensor", lambda t: t.matmul(psP[4][:], ones_r[:], srow_l[0:1, 512:1024], start=True, stop=True), r=[ones_r, srow_l], w=[psP[4]])
            S.step("vector", lambda v, RS2rep=RS2rep: v.tensor_copy(RS2rep[:], psP[4][:]), r=[psP[4]], w=[RS2rep])
            for i in range(NF):
                k = i % 2
                gl, ub, ci, cm, db = glb[k], ubb[k], cib[k], cm1b[k], dbb[k]
                S.step("ld", lambda s, i=i, tb=tb, gl=gl: s.dma_start(out=gl[:], in_=G_h[128 * i:128 * (i + 1), 512 * tb:512 * (tb + 1)]), r=[G_h], w=[gl], dma="ld")
                S.step("vector", lambda g, gl=gl, ub=ub, S2rep=S2rep: g.tensor_tensor(out=ub[:], in0=gl[:], in1=S2rep[:], op=ALU.mult), r=[gl, S2rep], w=[ub])
                S.step("vector", lambda v, ub=ub, ci=ci: v.tensor_copy(ci[:], ub[:]), r=[ub], w=[ci])
                S.step("vector", lambda g, ci=ci, cm=cm: g.tensor_scalar(out=cm[:], in0=ci[:], scalar1=-1.0, scalar2=None, op0=ALU.add), r=[ci], w=[cm])
                S.step("vector", lambda v, ub=ub, cm=cm, db=db: v.scalar_tensor_tensor(out=db[:], in0=ub[:], scalar=-1.0, in1=cm[:], op0=ALU.add, op1=ALU.subtract), r=[ub, cm], w=[db])
                S.step("vector", lambda v, db=db, cm=cm, ub=ub: v.scalar_tensor_tensor(out=ub[:], in0=db[:], scalar=0.5, in1=cm[:], op0=ALU.is_ge, op1=ALU.add), r=[db, cm], w=[ub])
                S.step("vector", lambda v, db=db, ub=ub, i=i, gq=gq: v.scalar_tensor_tensor(out=gq[i][:], in0=db[:], scalar=-0.5, in1=ub[:], op0=ALU.is_gt, op1=ALU.add), r=[db, ub], w=[gq[i]])
            for of in range(ND):
                ps = psP[of % 2]
                ev = evb[of % 3]
                def fmm(t, of=of, ps=ps, gq=gq):
                    return [t.matmul(ps[:], w2sb[i][:, 128 * of:128 * (of + 1)], gq[i][:], start=(i == 0), stop=(i == NF - 1))
                            for i in range(NF)]
                S.step("tensor", fmm, r=w2sb + gq, w=[ps])
                S.step("vector", lambda v, ps=ps, ev=ev, RS2rep=RS2rep: v.scalar_tensor_tensor(
                    out=ev[:], in0=ps[:], scalar=m2bc[:], in1=RS2rep[:], op0=ALU.mult, op1=ALU.mult), r=[ps, m2bc, RS2rep], w=[ev])
                S.step("st", lambda s, of=of, tb=tb, ev=ev: s.dma_start(
                    out=FFT_h[128 * of:128 * (of + 1), 512 * tb:512 * (tb + 1)], in_=ev[:]), r=[ev], w=[FFT_h], dma="st")

    with Phase() as ph:
        xb = [ph(nc.sbuf_tensor(f"p6_x{k}", [128, D], F32)) for k in range(2)]
        ab = [ph(nc.sbuf_tensor(f"p6_a{k}", [128, D], F32)) for k in range(3)]
        fb = [ph(nc.sbuf_tensor(f"p6_f{k}", [128, D], F32)) for k in range(3)]
        sb = [ph(nc.sbuf_tensor(f"p6_s{k}", [128, D], F32)) for k in range(3)]
        ob = [ph(nc.sbuf_tensor(f"p6_o{k}", [128, D], F32)) for k in range(2)]
        ATT_r = ATT_h.rearrange("(c p) t -> p c t", p=128)
        FFT_r = FFT_h.rearrange("(c p) t -> p c t", p=128)
        for tt in range(NT):
            k = tt % 2
            xs, os_ = xb[k], ob[k]
            av, fv, sv = ab[tt % 3], fb[tt % 3], sb[tt % 3]
            ps0, ps1 = psP[2 * k], psP[2 * k + 1]
            S.step("ld", lambda s, tt=tt, xs=xs: s.dma_start(out=xs[:], in_=x_h[128 * tt:128 * (tt + 1), :]), r=[x_h], w=[xs], dma="ld")
            S.step("ld", lambda s, tt=tt, av=av, fv=fv: [
                s.dma_start(out=av[:].rearrange("p (c t) -> p c t", c=ND), in_=ATT_r[:, :, 128 * tt:128 * (tt + 1)]),
                s.dma_start(out=fv[:].rearrange("p (c t) -> p c t", c=ND), in_=FFT_r[:, :, 128 * tt:128 * (tt + 1)])],
                r=[ATT_h, FFT_h], w=[av, fv], dma="ld", n=2)
            S.step("vector", lambda v, av=av, fv=fv, sv=sv: v.tensor_tensor(out=sv[:], in0=av[:], in1=fv[:], op=ALU.add), r=[av, fv], w=[sv])
            for dc in range(ND):
                ps = ps0 if dc < 4 else ps1
                S.step("tensor", lambda t, dc=dc, sv=sv, ps=ps: t.transpose(ps[:, 128 * (dc % 4):128 * (dc % 4 + 1)], sv[:, 128 * dc:128 * (dc + 1)], eye[:]), r=[sv, eye], w=[ps])
            S.step("vector", lambda v, xs=xs, os_=os_, ps0=ps0: v.tensor_tensor(out=os_[:, 0:512], in0=ps0[:], in1=xs[:, 0:512], op=ALU.add), r=[ps0, xs], w=[os_])
            S.step("vector", lambda v, xs=xs, os_=os_, ps1=ps1: v.tensor_tensor(out=os_[:, 512:1024], in0=ps1[:], in1=xs[:, 512:1024], op=ALU.add), r=[ps1, xs, os_], w=[os_])
            S.step("st", lambda s, tt=tt, os_=os_: s.dma_start(out=out_h[128 * tt:128 * (tt + 1), :], in_=os_[:]), r=[os_], w=[out_h], dma="st")

    top.close()
    return nc


def kernel(L0_attn_x, L0_attn_Wq, L0_attn_Wk, L0_attn_Wv, L0_attn_Wo,
           L0_ffn_w1, L0_ffn_conv_w, L0_ffn_conv_b, L0_ffn_w2):
    x = np.ascontiguousarray(np.asarray(L0_attn_x, dtype=np.float32))
    ws = {
        "wq": np.ascontiguousarray(np.asarray(L0_attn_Wq, np.float32)),
        "wk": np.ascontiguousarray(np.asarray(L0_attn_Wk, np.float32)),
        "wv": np.ascontiguousarray(np.asarray(L0_attn_Wv, np.float32)),
        "wo": np.ascontiguousarray(np.asarray(L0_attn_Wo, np.float32)),
        "w1": np.ascontiguousarray(np.asarray(L0_ffn_w1, np.float32)),
        "cw": np.ascontiguousarray(np.asarray(L0_ffn_conv_w, np.float32).reshape(DFF, 3)),
        "cb": np.ascontiguousarray(np.asarray(L0_ffn_conv_b, np.float32)),
        "w2": np.ascontiguousarray(np.asarray(L0_ffn_w2, np.float32)),
    }
    nc = build_program()
    in_maps = [dict(ws, x=np.ascontiguousarray(x[c])) for c in range(8)]
    res = run_bass_kernel_spmd(nc, in_maps, core_ids=list(range(8)))
    return np.stack([np.asarray(r["out"], np.float32) for r in res.results], axis=0)
